# version 04 (final)
# baseline (speedup 1.0000x reference)
import contextlib
import os
os.environ["TRN_ALLOW_ICI_ALL_TO_ALL"] = "1"
import numpy as np
import ml_dtypes
import concourse.bass as bass
import concourse.mybir as mybir
from concourse.bass_utils import run_bass_kernel_spmd

F32 = mybir.dt.float32
BF16 = mybir.dt.bfloat16
AF = mybir.ActivationFunctionType
ALU = mybir.AluOpType
AX = mybir.AxisListType
PI = float(np.pi)
MAGIC = 12582912.0
NCORES = 8
B, S, D = 4, 4096, 1024
TPB = 4
NT = B * TPB
ROWS = NT * 128

ENGS = ("sync", "tensor", "vector", "scalar", "gpsimd")
bfdt = ml_dtypes.bfloat16


class Chan:
    __slots__ = ("sem", "n", "name", "is_dma", "events")

    def __init__(self, name, sem=None, is_dma=False):
        self.sem, self.n, self.name = sem, 0, name
        self.is_dma = is_dma
        self.events = []


class K:
    AUTOSYNC = ("vector", "scalar", "gpsimd")

    def __init__(self, nc):
        self.nc = nc
        self.ops = {e: [] for e in ENGS}
        self.nchan = 0
        self._selfsem = {}

    def chan(self, name=None):
        self.nchan += 1
        return Chan(name or f"ch{self.nchan}")

    def dchan(self, name=None):
        self.nchan += 1
        sem = self.nc.alloc_semaphore(name=name or f"dch{self.nchan}")
        return Chan(name or f"dch{self.nchan}", sem, is_dma=True)

    def _self(self, eng):
        if eng not in self._selfsem:
            self._selfsem[eng] = [self.nc.alloc_semaphore(name=f"self_{eng}"),
                                  0]
        return self._selfsem[eng]

    def do(self, eng, fn, inc=None, inc16=None, auto=True):
        if inc16 is not None:
            assert inc is None
            inc16.n += 16
            def wrapped(e, _fn=fn, _s=inc16.sem):
                return _fn(e).then_inc(_s, 16)
            self.ops[eng].append(wrapped)
            return
        chans = ([] if inc is None else
                 (inc if isinstance(inc, (list, tuple)) else [inc]))
        ss = self._self(eng)
        serialize = auto and eng in self.AUTOSYNC
        if serialize:
            prev = ss[1]
            if prev > 0:
                self.ops[eng].append(
                    lambda e, _s=ss[0], _v=prev: e.wait_ge(_s, _v))
        if eng == "tensor" and chans:
            self.ops[eng].append(lambda e, _fn=fn: _fn(e))
            ss[1] += 1
            cnt = ss[1]
            for ch in chans:
                assert not ch.is_dma
                ch.n += 1
                ch.events.append((ss[0], cnt))
            self.ops[eng].append(
                lambda e, _s=ss[0]: e.drain().then_inc(_s, 1))
        elif serialize or chans:
            ss[1] += 1
            cnt = ss[1]
            for ch in chans:
                assert not ch.is_dma
                ch.n += 1
                ch.events.append((ss[0], cnt))
            def wrapped(e, _fn=fn, _s=ss[0], _c=cnt):
                return _fn(e).then_inc(_s, 1)
            self.ops[eng].append(wrapped)
        else:
            self.ops[eng].append(lambda e, _fn=fn: _fn(e))

    def wait(self, eng, ch, val=None):
        v = ch.n if val is None else val
        if v <= 0:
            return
        if ch.is_dma:
            self.ops[eng].append(lambda e, _s=ch.sem, _v=v: e.wait_ge(_s, _v))
            return
        need = {}
        for sem, cnt in ch.events[:v]:
            if need.get(id(sem), (None, 0))[1] < cnt:
                need[id(sem)] = (sem, cnt)
        for sem, cnt in need.values():
            self.ops[eng].append(lambda e, _s=sem, _v=cnt: e.wait_ge(_s, _v))

    def dma(self, eng, out, in_, ch, **kw):
        assert ch.is_dma
        self.wait(eng, ch, ch.n)
        self.do(eng, lambda e, _o=out, _i=in_, _kw=kw:
                e.dma_start(out=_o, in_=_i, **_kw), inc16=ch)

    def barrier(self):
        bch = {e: self.chan() for e in ENGS}
        for e in ENGS:
            self.do(e, lambda en: en.nop(), inc=bch[e])
        for e in ENGS:
            for e2 in ENGS:
                if e2 != e:
                    self.wait(e, bch[e2])

    def play(self, block):
        for ename in ENGS:
            def runner(eng, _fns=self.ops[ename]):
                for f in _fns:
                    f(eng)
            getattr(block, ename)(runner)


class Ctx:

    def __init__(self, nc, k, stk):
        self.nc, self.k, self.stk = nc, k, stk
        self.inp = {}
        self.dram_n = 0

    def dram(self, shape, dtype, name=None, space="Local"):
        self.dram_n += 1
        return self.nc.dram_tensor(name or f"scr{self.dram_n}", list(shape),
                                   dtype, addr_space=space).ap()

    def sb(self, stk, shape, dtype, name=None):
        return stk.enter_context(self.nc.sbuf_tensor(name, list(shape), dtype))

    def ps(self, stk, shape, dtype=F32, name=None):
        return stk.enter_context(self.nc.psum_tensor(name, list(shape), dtype))


def emit_prelude(k, C, stk):
    C.identb = C.sb(stk, [128, 128], BF16, "identb")
    C.identf = C.sb(stk, [128, 128], F32, "identf")
    C.trib = C.sb(stk, [128, 128], BF16, "trib")
    C.onesb_col = C.sb(stk, [128, 1], BF16, "onesb_col")
    C.onesb_row = C.sb(stk, [1, 128], BF16, "onesb_row")
    C.ld0 = k.dchan("ld0")
    C.ld0g = k.dchan("ld0g")
    C.const_done = k.chan("const_done")
    k.dma("sync", C.identf[:, :], C.inp["ident128"][:, :], C.ld0)
    k.dma("gpsimd", C.identb[:, :], C.inp["ident128"][:, :], C.ld0g)
    g = "gpsimd"
    k.do(g, lambda e: e.memset(C.trib[:, :], 0.0))
    k.do(g, lambda e: e.affine_select(
        out=C.trib[:, :], in_=C.trib[:, :], compare_op=ALU.is_gt,
        fill=1.0, base=0, pattern=[[-1, 128]], channel_multiplier=1))
    k.do(g, lambda e: e.memset(C.onesb_col[:, :], 1.0))
    k.do(g, lambda e: e.memset(C.onesb_row[:, :], 1.0), inc=C.const_done)
    for e in ENGS:
        k.wait(e, C.const_done)
        k.wait(e, C.ld0)
        k.wait(e, C.ld0g)


def ln_tile(k, sx, xln, sq, cols, ch_a, ch_v, ch_done=None, out_f32=None):
    v, a = "vector", "scalar"
    m, ms, var = cols[:, 0:1], cols[:, 1:2], cols[:, 2:3]
    std, rstd = cols[:, 3:4], cols[:, 4:5]
    k.do(v, lambda e: e.reduce_sum(m, sx[:, :], axis=AX.X))
    k.do(v, lambda e: e.tensor_scalar(m, m, 1.0 / 1024.0, None, ALU.mult))
    k.do(v, lambda e: e.tensor_scalar(sq[:, :], sx[:, :], m, None,
                                      ALU.subtract))
    k.do(v, lambda e: e.tensor_tensor(sq[:, :], sq[:, :], sq[:, :], ALU.mult))
    k.do(v, lambda e: e.reduce_sum(ms, sq[:, :], axis=AX.X))
    k.do(v, lambda e: e.tensor_scalar(var, ms, 1.0 / 1024.0, 1e-5,
                                      ALU.mult, ALU.add), inc=ch_v)
    k.wait(a, ch_v)
    k.do(a, lambda e: e.activation(std, var, AF.Sqrt), inc=ch_a)
    k.wait(v, ch_a)
    k.do(v, lambda e: e.reciprocal(rstd, std))
    if out_f32 is not None:
        k.do(v, lambda e: e.tensor_scalar(out_f32[:, :], sx[:, :], m, rstd,
                                          ALU.subtract, ALU.mult))
        k.do(v, lambda e: e.tensor_copy(xln[:, :], out_f32[:, :]),
             inc=ch_done)
    else:
        k.do(v, lambda e: e.tensor_scalar(xln[:, :], sx[:, :], m, rstd,
                                          ALU.subtract, ALU.mult),
             inc=ch_done)


def emit_l3ffn(k, C, xin, xout):
    with contextlib.ExitStack() as bs:
        sx = [C.sb(bs, [128, 1024], F32, f"l3f_sx{i}") for i in range(8)]
        xln = C.sb(bs, [128, 1024], BF16, "l3f_xln")
        sq = C.sb(bs, [128, 1024], F32, "l3f_sq")
        cols = C.sb(bs, [128, 8], F32, "l3f_cols")
        xT = [C.sb(bs, [128, 1024], BF16, f"l3f_xT{c}") for c in range(8)]
        hT = C.sb(bs, [128, 32 * 1024], BF16, "l3f_hT")
        relu_t = C.sb(bs, [128, 512], F32, "l3f_relu")
        wfc = [C.sb(bs, [128, 1024], BF16, f"l3f_wfc{i}") for i in range(2)]
        wpj = [[C.sb(bs, [128, 1024], BF16, f"l3f_wpj{g}_{i}")
                for i in range(8)] for g in range(2)]
        ytile = C.sb(bs, [128, 512], F32, "l3f_y")
        mem = C.sb(bs, [128, 512], F32, "l3f_mem")
        rst = C.sb(bs, [128, 512], F32, "l3f_rst")
        spk = C.sb(bs, [128, 512], F32, "l3f_spk")
        xnew = C.sb(bs, [128, 512], F32, "l3f_xnew")

        ch_x = [k.dchan() for _ in range(8)]
        ch_lna, ch_lnv = k.chan(), k.chan()
        ch_ln = k.chan()
        ch_tr = k.chan()
        ch_trev = k.chan()
        ch_w = [k.dchan(), k.dchan()]
        ch_wp = [k.dchan(), k.dchan()]
        ch_hmm = k.chan()
        ch_hev = k.chan()
        ch_ymm = k.chan()
        ch_yftg = k.chan()
        ch_yev = k.chan()
        ch_st = k.dchan()

        wfcT = C.inp["l3f_wfcT"]
        wpjT = C.inp["l3f_wpjT"]

        for oct_ in range(2):
            tiles = list(range(oct_ * 8, oct_ * 8 + 8))
            k.wait("sync", ch_yev, ch_yev.n)
            for i, ti in enumerate(tiles):
                k.dma("sync", sx[i][:, :], xin[ti * 128:(ti + 1) * 128, :],
                      ch_x[i])

            with contextlib.ExitStack() as psA:
                ptr = [C.ps(psA, [128, 128], BF16, f"l3f_ptr{i}_{oct_}")
                       for i in range(2)]
                ph = [C.ps(psA, [128, 512], F32, f"l3f_ph{i}_{oct_}")
                      for i in range(2)]
                k.wait("tensor", ch_yev, ch_yev.n)
                k.wait("vector", ch_hmm, ch_hmm.n)

                for i, ti in enumerate(tiles):
                    k.wait("vector", ch_x[i], ch_x[i].n)
                    if i > 0 or oct_ > 0:
                        k.wait("vector", ch_tr, ch_tr.n)
                    ln_tile(k, sx[i], xln, sq, cols, ch_lna, ch_lnv,
                            ch_done=ch_ln)
                    k.wait("tensor", ch_ln, ch_ln.n)
                    for c in range(8):
                        sl = c % 2
                        if c >= 2 or i > 0 or oct_ > 0:
                            k.wait("tensor", ch_trev, ch_trev.n - 1)
                        k.do("tensor", lambda e, _c=c, _sl=sl: e.transpose(
                            ptr[_sl][:, :], xln[:, _c * 128:(_c + 1) * 128],
                            C.identb[:, :]), inc=ch_tr)
                        k.wait("vector", ch_tr, ch_tr.n)
                        k.do("vector", lambda e, _c=c, _sl=sl, _i=i:
                             e.tensor_copy(xT[_c][:, _i * 128:(_i + 1) * 128],
                                           ptr[_sl][:, :]), inc=ch_trev)
                k.wait("scalar", ch_ymm, ch_ymm.n)
                for ft in range(32):
                    wb = ft % 2
                    gft = oct_ * 32 + ft
                    if gft >= 2:
                        k.wait("gpsimd", ch_hmm, gft - 1)
                    k.dma("gpsimd",
                          wfc[wb][:, :].rearrange("p (c f) -> p c f", f=128),
                          wfcT.rearrange("(c p) f -> p c f", p=128)
                          [:, :, ft * 128:(ft + 1) * 128], ch_w[wb])
                    k.wait("tensor", ch_w[wb], ch_w[wb].n)
                    k.wait("tensor", ch_trev, ch_trev.n)
                    for half in range(2):
                        k.wait("tensor", ch_hev, ch_hev.n - 1 + half)
                        for c in range(8):
                            k.do("tensor",
                                 lambda e, _h=half, _wb=wb, _c=c:
                                 e.matmul(ph[_h][:, :],
                                          wfc[_wb][:, _c * 128:(_c + 1) * 128],
                                          xT[_c][:, _h * 512:(_h + 1) * 512],
                                          start=(_c == 0), stop=(_c == 7)),
                                 inc=(ch_hmm if (c == 7 and half == 1)
                                      else None))
                    k.wait("scalar", ch_hmm, ch_hmm.n)
                    for half in range(2):
                        k.do("scalar", lambda e, _h=half: e.activation(
                            relu_t[:, :], ph[_h][:, :], AF.Relu))
                        k.do("scalar", lambda e, _h=half, _ft=ft: e.activation(
                            hT[:, _ft * 1024 + _h * 512:
                               _ft * 1024 + (_h + 1) * 512],
                            relu_t[:, :], AF.Square), inc=ch_hev)

            with contextlib.ExitStack() as psB:
                py = [C.ps(psB, [128, 512], F32, f"l3f_py{i}_{oct_}")
                      for i in range(8)]
                for half in range(2):
                    for ftg in range(4):
                        wg = ftg % 2
                        gftg = oct_ * 8 + half * 4 + ftg
                        if gftg >= 2:
                            k.wait("gpsimd", ch_yftg, gftg - 1)
                        for f in range(8):
                            k.dma("gpsimd", wpj[wg][f][:, :],
                                  wpjT[(ftg * 8 + f) * 128:
                                       (ftg * 8 + f + 1) * 128, :],
                                  ch_wp[wg])
                        k.wait("tensor", ch_wp[wg], ch_wp[wg].n)
                        if ftg == 0:
                            k.wait("tensor", ch_yev, ch_yev.n)
                            k.wait("tensor", ch_hev, ch_hev.n)
                        for it in range(4):
                            for dc in range(2):
                                pb = py[it * 2 + dc]
                                for f in range(8):
                                    ft = ftg * 8 + f
                                    k.do("tensor",
                                         lambda e, _pb=pb, _wg=wg, _f=f,
                                         _ft=ft, _it=it, _half=half, _dc=dc:
                                         e.matmul(
                                             _pb[:, :],
                                             hT[:, _ft * 1024 +
                                                (_half * 4 + _it) * 128:
                                                _ft * 1024 +
                                                (_half * 4 + _it + 1) * 128],
                                             wpj[_wg][_f][:,
                                                          _dc * 512:(_dc + 1) * 512],
                                             start=(_ft == 0), stop=(_ft == 31)),
                                         inc=(ch_ymm if ft == 31 else None))
                        k.do("tensor", lambda e: e.nop(), inc=ch_yftg)
                    base = ch_ymm.n - 8
                    for it in range(4):
                        for dc in range(2):
                            j = it * 2 + dc
                            pb = py[j]
                            v = "vector"
                            k.wait(v, ch_ymm, base + j + 1)
                            k.do(v, lambda e, _pb=pb:
                                 e.tensor_copy(ytile[:, :], _pb[:, :]))
                            k.do(v, lambda e: e.tensor_copy(mem[:, :],
                                                            ytile[:, :]))
                            k.do(v, lambda e: e.tensor_scalar(
                                spk[:, :], ytile[:, :], 1.0, None, ALU.is_gt))
                            for _ in range(4):
                                k.do(v, lambda e: e.tensor_scalar(
                                    rst[:, :], mem[:, :], 1.0, None, ALU.is_gt))
                                k.do(v, lambda e: e.scalar_tensor_tensor(
                                    mem[:, :], mem[:, :], 0.9, ytile[:, :],
                                    ALU.mult, ALU.add))
                                k.do(v, lambda e: e.tensor_tensor(
                                    mem[:, :], mem[:, :], rst[:, :],
                                    ALU.subtract))
                                k.do(v, lambda e: e.tensor_scalar(
                                    rst[:, :], mem[:, :], 1.0, None, ALU.is_gt))
                                k.do(v, lambda e: e.tensor_tensor(
                                    spk[:, :], spk[:, :], rst[:, :], ALU.add))
                            import os as _os
                            if _os.environ.get("L3DBG"):
                                k.do(v, lambda e: e.tensor_copy(
                                    xnew[:, :], ytile[:, :]), inc=ch_yev)
                            else:
                                k.do(v, lambda e, _it=it, _h=half, _dc=dc:
                                     e.scalar_tensor_tensor(
                                         xnew[:, :], spk[:, :], 0.2,
                                         sx[_h * 4 + _it][:, _dc * 512:(_dc + 1) * 512],
                                         ALU.mult, ALU.add), inc=ch_yev)
                            k.wait("sync", ch_yev, ch_yev.n)
                            ti = tiles[half * 4 + it]
                            k.dma("sync", xout[ti * 128:(ti + 1) * 128,
                                               dc * 512:(dc + 1) * 512],
                                  xnew[:, :], ch_st)
                            k.wait("vector", ch_st, ch_st.n)
        k.wait("sync", ch_st, ch_st.n)



def emit_l1ffn(k, C, xin, xout):
    with contextlib.ExitStack() as bs:
        wmats = {n: C.sb(bs, [128, 8192], BF16, f"l1fs_{n}")
                 for n in ("wfT", "wiT", "wcT", "ufT", "uiT", "ucT")}
        brows = {n: C.sb(bs, [1, 1024], BF16, f"l1fs_{n}r")
                 for n in ("bf", "bi", "bc", "bo", "inc")}
        ones_pos = C.sb(bs, [1, 512], BF16, "l1f_onespos")
        sx = C.sb(bs, [128, 1024], F32, "l1f_sx")
        xln = C.sb(bs, [128, 1024], BF16, "l1f_xln")
        sq = C.sb(bs, [128, 1024], F32, "l1f_sq")
        cols = C.sb(bs, [128, 8], F32, "l1f_cols")
        xTall = C.sb(bs, [128, 16 * 1024], BF16, "l1f_xT")
        cF = C.sb(bs, [128, 4096], F32, "l1f_cF")
        cFb = C.sb(bs, [128, 4096], BF16, "l1f_cFb")
        gf = C.sb(bs, [128, 512], F32, "l1f_gf")
        gi_ = C.sb(bs, [128, 512], F32, "l1f_gi")
        gg = C.sb(bs, [128, 512], F32, "l1f_gg")
        t1 = C.sb(bs, [128, 512], F32, "l1f_t1")
        hh = C.sb(bs, [128, 1024], F32, "l1f_hh")

        ptr = [C.ps(bs, [128, 128], BF16, f"l1f_ptr{i}") for i in range(2)]
        pg = [C.ps(bs, [128, 512], F32, f"l1f_pg{i}") for i in range(3)]
        pho = [C.ps(bs, [128, 512], F32, f"l1f_pho{i}") for i in range(2)]

        ch_wl = k.dchan()
        ch_x = k.dchan()
        ch_lna, ch_lnv, ch_ln = k.chan(), k.chan(), k.chan()
        ch_tr, ch_trev = k.chan(), k.chan()
        ch_gmm, ch_gev, ch_cup = k.chan(), k.chan(), k.chan()
        ch_hmm, ch_hev = k.chan(), k.chan()
        ch_st = k.dchan()

        for n, t in wmats.items():
            k.dma("gpsimd", t[:, :].rearrange("p (c f) -> p c f", f=1024),
                  C.inp[f"l1f_{n}"].rearrange("(c p) f -> p c f", p=128),
                  ch_wl)
        for n in ("bf", "bi", "bc", "bo", "inc"):
            k.dma("gpsimd", brows[n][:, :], C.inp[f"l1f_{n}"][:, :], ch_wl)
        ch_ms = k.chan()
        k.do("gpsimd", lambda e: e.memset(ones_pos[:, :], 1.0), inc=ch_ms)
        k.wait("tensor", ch_wl)
        k.wait("tensor", ch_ms)
        k.wait("vector", ch_wl)
        woT = wmats["wfT"]
        uoT = wmats["wiT"]

        for ci in range(8):
            k.do("tensor", lambda e, _ci=ci: e.matmul(
                pg[_ci % 3][:, :],
                brows["inc"][0:1, _ci * 128:(_ci + 1) * 128],
                ones_pos[0:1, :], start=True, stop=True), inc=ch_gmm)
            k.wait("vector", ch_gmm, ch_gmm.n)
            k.do("vector", lambda e, _ci=ci: e.tensor_copy(
                cF[:, _ci * 512:(_ci + 1) * 512], pg[_ci % 3][:, :]),
                 inc=ch_gev)
            if ci % 3 == 2 or ci == 7:
                k.wait("tensor", ch_gev, ch_gev.n)
        for ci in range(8):
            k.do("vector", lambda e, _ci=ci: e.tensor_copy(
                cFb[:, _ci * 512:(_ci + 1) * 512],
                cF[:, _ci * 512:(_ci + 1) * 512]), inc=ch_cup)

        GATES = (("wfT", "ufT", "bf", AF.Sigmoid, gf),
                 ("wiT", "uiT", "bi", AF.Sigmoid, gi_),
                 ("wcT", "ucT", "bc", AF.Tanh, gg))

        for t in range(4):
            for j in range(4):
                ti = t * 4 + j
                k.wait("sync", ch_ln, ch_ln.n)
                k.wait("sync", ch_x, ch_x.n)
                k.dma("sync", sx[:, :], xin[ti * 128:(ti + 1) * 128, :], ch_x)
                k.wait("vector", ch_x, ch_x.n)
                k.wait("vector", ch_tr, ch_tr.n)
                ln_tile(k, sx, xln, sq, cols, ch_lna, ch_lnv, ch_done=ch_ln)
                k.wait("tensor", ch_ln, ch_ln.n)
                for c in range(8):
                    sl = c % 2
                    k.wait("tensor", ch_trev, ch_trev.n)
                    k.do("tensor", lambda e, _c=c, _sl=sl: e.transpose(
                        ptr[_sl][:, :], xln[:, _c * 128:(_c + 1) * 128],
                        C.identb[:, :]), inc=ch_tr)
                    k.wait("vector", ch_tr, ch_tr.n)
                    k.do("vector", lambda e, _c=c, _sl=sl, _t=t, _j=j:
                         e.tensor_copy(
                             xTall[:, (_t * 8 + _c) * 512 + _j * 128:
                                   (_t * 8 + _c) * 512 + (_j + 1) * 128],
                             ptr[_sl][:, :]), inc=ch_trev)
            k.wait("tensor", ch_trev, ch_trev.n)
            k.wait("tensor", ch_cup, ch_cup.n)
            for ci in range(8):
                k.wait("tensor", ch_gev, ch_gev.n)
                for gidx, (wn, un, bn, _af, _dst) in enumerate(GATES):
                    pb = pg[gidx]
                    for c in range(8):
                        k.do("tensor", lambda e, _pb=pb, _wn=wn, _c=c,
                             _ci=ci, _t=t: e.matmul(
                                 _pb[:, :],
                                 wmats[_wn][:, _c * 1024 + _ci * 128:
                                            _c * 1024 + (_ci + 1) * 128],
                                 xTall[:, (_t * 8 + _c) * 512:
                                       (_t * 8 + _c) * 512 + 512],
                                 start=(_c == 0), stop=False))
                    for c in range(8):
                        k.do("tensor", lambda e, _pb=pb, _un=un, _c=c,
                             _ci=ci: e.matmul(
                                 _pb[:, :],
                                 wmats[_un][:, _c * 1024 + _ci * 128:
                                            _c * 1024 + (_ci + 1) * 128],
                                 cFb[:, _c * 512:(_c + 1) * 512],
                                 start=False, stop=False))
                    k.do("tensor", lambda e, _pb=pb, _bn=bn, _ci=ci:
                         e.matmul(_pb[:, :],
                                  brows[_bn][0:1, _ci * 128:(_ci + 1) * 128],
                                  ones_pos[0:1, :],
                                  start=False, stop=True), inc=ch_gmm)
                k.wait("scalar", ch_gmm, ch_gmm.n)
                k.wait("scalar", ch_cup, ch_cup.n)
                for gidx, (_wn, _un, _bn, af, dst) in enumerate(GATES):
                    k.do("scalar", lambda e, _g=gidx, _af=af, _dst=dst:
                         e.activation(_dst[:, :], pg[_g][:, :], _af),
                         inc=ch_gev)
                k.wait("vector", ch_gev, ch_gev.n)
                k.do("vector", lambda e, _ci=ci: e.tensor_tensor(
                    t1[:, :], gf[:, :], cF[:, _ci * 512:(_ci + 1) * 512],
                    ALU.mult))
                k.do("vector", lambda e: e.tensor_tensor(
                    gf[:, :], gi_[:, :], gg[:, :], ALU.mult))
                k.do("vector", lambda e, _ci=ci: e.tensor_tensor(
                    cF[:, _ci * 512:(_ci + 1) * 512], t1[:, :], gf[:, :],
                    ALU.add), inc=ch_cup)
            k.wait("vector", ch_gmm, ch_gmm.n)
            for ci in range(8):
                k.do("vector", lambda e, _ci=ci: e.tensor_copy(
                    cFb[:, _ci * 512:(_ci + 1) * 512],
                    cF[:, _ci * 512:(_ci + 1) * 512]), inc=ch_cup)
            k.wait("tensor", ch_cup, ch_cup.n)

        k.wait("gpsimd", ch_gmm, ch_gmm.n)
        ch_wl2 = k.dchan()
        k.dma("gpsimd", woT[:, :].rearrange("p (c f) -> p c f", f=1024),
              C.inp["l1f_woT"].rearrange("(c p) f -> p c f", p=128), ch_wl2)
        k.dma("gpsimd", uoT[:, :].rearrange("p (c f) -> p c f", f=1024),
              C.inp["l1f_uoT"].rearrange("(c p) f -> p c f", p=128), ch_wl2)
        k.wait("tensor", ch_wl2)
        for t in range(4):
            for j in range(4):
                ti = t * 4 + j
                k.wait("tensor", ch_hev, ch_hev.n)
                for h in range(2):
                    pb = pho[h]
                    for c in range(8):
                        k.do("tensor", lambda e, _pb=pb, _c=c, _h=h, _t=t,
                             _j=j: e.matmul(
                                 _pb[:, :],
                                 xTall[:, (_t * 8 + _c) * 512 + _j * 128:
                                       (_t * 8 + _c) * 512 + (_j + 1) * 128],
                                 woT[:, _c * 1024 + _h * 512:
                                     _c * 1024 + _h * 512 + 512],
                                 start=(_c == 0), stop=False))
                    for c in range(8):
                        k.do("tensor", lambda e, _pb=pb, _c=c, _h=h, _j=j:
                             e.matmul(_pb[:, :],
                                      cFb[:, _c * 512 + _j * 128:
                                          _c * 512 + (_j + 1) * 128],
                                      wmats["wcT" if False else "wiT"]
                                      [:, _c * 1024 + _h * 512:
                                       _c * 1024 + _h * 512 + 512]
                                      if False else
                                      uoT[:, _c * 1024 + _h * 512:
                                          _c * 1024 + _h * 512 + 512],
                                      start=False, stop=False))
                    k.do("tensor", lambda e, _pb=pb, _h=h:
                         e.matmul(_pb[:, :], C.onesb_row[:, :],
                                  brows["bo"][0:1, _h * 512:_h * 512 + 512],
                                  start=False, stop=True), inc=ch_hmm)
                k.wait("scalar", ch_hmm, ch_hmm.n)
                for h in range(2):
                    k.do("scalar", lambda e, _h=h: e.activation(
                        hh[:, _h * 512:(_h + 1) * 512], pho[_h][:, :],
                        AF.Tanh), inc=ch_hev)
                k.wait("sync", ch_st, ch_st.n)
                k.wait("sync", ch_x, ch_x.n)
                k.wait("sync", ch_ln, ch_ln.n)
                k.dma("sync", sx[:, :], xin[ti * 128:(ti + 1) * 128, :], ch_x)
                k.wait("vector", ch_x, ch_x.n)
                k.wait("vector", ch_hev, ch_hev.n)
                k.do("vector", lambda e: e.tensor_tensor(
                    hh[:, :], hh[:, :], sx[:, :], ALU.add), inc=ch_cup)
                k.wait("sync", ch_cup, ch_cup.n)
                k.dma("sync", xout[ti * 128:(ti + 1) * 128, :], hh[:, :],
                      ch_st)
                k.wait("scalar", ch_st, ch_st.n)
        k.wait("sync", ch_st, ch_st.n)


def emit_moe(k, C, xin, xout, pref, F, has_br):
    NFT = F // 128
    NFTG = NFT // 8
    with contextlib.ExitStack() as bs:
        sx = C.sb(bs, [128, 1024], F32, f"{pref}_sx")
        xln = C.sb(bs, [128, 1024], BF16, f"{pref}_xln")
        sq = C.sb(bs, [128, 1024], F32, f"{pref}_sq")
        cols = C.sb(bs, [128, 8], F32, f"{pref}_cols")
        xT = [C.sb(bs, [128, 1024], BF16, f"{pref}_xT{c}") for c in range(8)]
        hT = C.sb(bs, [128, NFT * 1024], BF16, f"{pref}_hT")
        w1s = [C.sb(bs, [128, 1024], BF16, f"{pref}_w1s{i}") for i in range(2)]
        w2s = [[C.sb(bs, [128, 1024], BF16, f"{pref}_w2s{g}_{i}")
                for i in range(8)] for g in range(2)]
        wrt = C.sb(bs, [128, 64], BF16, f"{pref}_wrt")
        brr = C.sb(bs, [1, 8], BF16, f"{pref}_brr")
        b1c = C.sb(bs, [128, 32], F32, f"{pref}_b1c")
        b2r = C.sb(bs, [1, 1024], BF16, f"{pref}_b2r")
        lg = C.sb(bs, [128, 8], F32, f"{pref}_lg")
        expd = C.sb(bs, [128, 8], F32, f"{pref}_expd")
        rcols = C.sb(bs, [128, 4], F32, f"{pref}_rcols")
        mw = C.sb(bs, [128, 64], F32, f"{pref}_mw")
        oacc = [C.sb(bs, [128, 1024], F32, f"{pref}_oa{i}") for i in range(8)]

        ch_x = k.dchan()
        ch_lna, ch_lnv, ch_ln = k.chan(), k.chan(), k.chan()
        ch_tr, ch_trev = k.chan(), k.chan()
        ch_w1 = [k.dchan(), k.dchan()]
        ch_w2 = [k.dchan(), k.dchan()]
        ch_wr = k.dchan()
        ch_rt = k.chan()
        ch_hmm, ch_hev = k.chan(), k.chan()
        ch_ymm, ch_yev = k.chan(), k.chan()
        ch_yf = k.chan()
        ch_res = k.chan()
        ch_st = k.dchan()

        k.dma("gpsimd", wrt[:, :].rearrange("p (c f) -> p c f", f=8),
              C.inp[f"{pref}_wr"].rearrange("(c p) f -> p c f", p=128), ch_wr)
        if has_br:
            k.dma("gpsimd", brr[:, :], C.inp[f"{pref}_br"][:, :], ch_wr)
        k.wait("tensor", ch_wr)

        for oct_ in range(2):
            tiles = list(range(oct_ * 8, oct_ * 8 + 8))
            with contextlib.ExitStack() as psA:
                ptr = [C.ps(psA, [128, 128], BF16, f"{pref}_ptr{i}_{oct_}")
                       for i in range(2)]
                pr = C.ps(psA, [128, 8], F32, f"{pref}_pr_{oct_}")
                for i, ti in enumerate(tiles):
                    k.wait("sync", ch_ln, ch_ln.n)
                    k.wait("sync", ch_res, ch_res.n)
                    k.wait("sync", ch_x, ch_x.n)
                    k.dma("sync", sx[:, :],
                          xin[ti * 128:(ti + 1) * 128, :], ch_x)
                    k.wait("vector", ch_x, ch_x.n)
                    k.wait("vector", ch_tr, ch_tr.n)
                    ln_tile(k, sx, xln, sq, cols, ch_lna, ch_lnv,
                            ch_done=ch_ln)
                    k.wait("tensor", ch_ln, ch_ln.n)
                    for c in range(8):
                        sl = c % 2
                        k.wait("tensor", ch_trev, ch_trev.n)
                        k.do("tensor", lambda e, _c=c, _sl=sl: e.transpose(
                            ptr[_sl][:, :], xln[:, _c * 128:(_c + 1) * 128],
                            C.identb[:, :]), inc=ch_tr)
                        k.wait("vector", ch_tr, ch_tr.n)
                        k.do("vector", lambda e, _c=c, _sl=sl, _i=i:
                             e.tensor_copy(
                                 xT[_c][:, _i * 128:(_i + 1) * 128],
                                 ptr[_sl][:, :]), inc=ch_trev)
                    k.wait("tensor", ch_rt, ch_rt.n)
                    k.wait("tensor", ch_trev, ch_trev.n)
                    for c in range(8):
                        k.do("tensor", lambda e, _c=c, _i=i: e.matmul(
                            pr[:, :], xT[_c][:, _i * 128:(_i + 1) * 128],
                            wrt[:, _c * 8:(_c + 1) * 8],
                            start=(_c == 0), stop=(not has_br and _c == 7)))
                    if has_br:
                        k.do("tensor", lambda e: e.matmul(
                            pr[:, :], C.onesb_row[:, :], brr[0:1, :],
                            start=False, stop=True), inc=ch_rt)
                    else:
                        k.do("tensor", lambda e: e.nop(), inc=ch_rt,
                             auto=False)
                    rmax, nmax, sume, wgt = (rcols[:, 0:1], rcols[:, 1:2],
                                             rcols[:, 2:3], rcols[:, 3:4])
                    k.wait("vector", ch_rt, ch_rt.n)
                    k.do("vector", lambda e: e.tensor_copy(lg[:, :], pr[:, :]),
                         inc=ch_rt)
                    k.do("vector", lambda e: e.reduce_max(rmax, lg[:, :],
                                                          axis=AX.X))
                    k.do("vector", lambda e: e.tensor_scalar(
                        nmax, rmax, -1.0, None, ALU.mult), inc=ch_lnv)
                    k.wait("scalar", ch_lnv, ch_lnv.n)
                    k.do("scalar", lambda e: e.activation(
                        expd[:, :], lg[:, :], AF.Exp, bias=nmax,
                        accum_out=sume), inc=ch_lna)
                    k.wait("vector", ch_lna, ch_lna.n)
                    k.do("vector", lambda e: e.reciprocal(wgt, sume))
                    k.do("vector", lambda e, _i=i: e.tensor_scalar(
                        mw[:, _i * 8:(_i + 1) * 8], lg[:, :], rmax, None,
                        ALU.is_ge))
                    k.do("vector", lambda e, _i=i: e.tensor_scalar(
                        mw[:, _i * 8:(_i + 1) * 8],
                        mw[:, _i * 8:(_i + 1) * 8], wgt, None,
                        ALU.mult), inc=ch_rt)

            k.wait("vector", ch_st, ch_st.n)
            for e_ in range(8):
                k.dma("gpsimd", b1c[:, 0:NFT],
                      C.inp[f"{pref}_b1t"][e_], ch_w1[0])
                with contextlib.ExitStack() as psH:
                    ph = [C.ps(psH, [128, 512], F32, f"{pref}_ph{i}_{oct_}_{e_}")
                          for i in range(2)]
                    for ft in range(NFT):
                        wb = ft % 2
                        gft = (oct_ * 8 + e_) * NFT + ft
                        if gft >= 2:
                            k.wait("gpsimd", ch_hmm, gft - 1)
                        k.dma("gpsimd",
                              w1s[wb][:, :].rearrange("p (c f) -> p c f", f=128),
                              C.inp[f"{pref}_w1"][e_].rearrange(
                                  "(c p) f -> p c f", p=128)
                              [:, :, ft * 128:(ft + 1) * 128], ch_w1[wb])
                        k.wait("tensor", ch_w1[wb], ch_w1[wb].n)
                        if ft == 0:
                            k.wait("tensor", ch_trev, ch_trev.n)
                            k.wait("tensor", ch_yev, ch_yev.n)
                        for half in range(2):
                            k.wait("tensor", ch_hev, ch_hev.n - 1 + half)
                            for c in range(8):
                                k.do("tensor", lambda e, _h=half, _wb=wb,
                                     _c=c: e.matmul(
                                         ph[_h][:, :],
                                         w1s[_wb][:, _c * 128:(_c + 1) * 128],
                                         xT[_c][:, _h * 512:(_h + 1) * 512],
                                         start=(_c == 0), stop=(_c == 7)),
                                     inc=(ch_hmm if (c == 7 and half == 1)
                                          else None))
                        k.wait("scalar", ch_hmm, ch_hmm.n)
                        k.wait("scalar", ch_w1[0], ch_w1[0].n)
                        for half in range(2):
                            k.do("scalar", lambda e, _h=half, _ft=ft:
                                 e.activation(
                                     hT[:, _ft * 1024 + _h * 512:
                                        _ft * 1024 + (_h + 1) * 512],
                                     ph[_h][:, :], AF.Relu,
                                     bias=b1c[:, _ft:_ft + 1]), inc=ch_hev)
                with contextlib.ExitStack() as psY:
                    py = [C.ps(psY, [128, 512], F32,
                               f"{pref}_py{i}_{oct_}_{e_}") for i in range(8)]
                    k.dma("gpsimd", b2r[:, :],
                          C.inp[f"{pref}_b2"][e_:e_ + 1, :], ch_w2[0])
                    for half in range(2):
                        for ftg in range(NFTG):
                            wg = ftg % 2
                            gftg = ((oct_ * 8 + e_) * 2 + half) * NFTG + ftg
                            if gftg >= 2:
                                k.wait("gpsimd", ch_yf, gftg - 1)
                            for f in range(8):
                                k.dma("gpsimd", w2s[wg][f][:, :],
                                      C.inp[f"{pref}_w2"][e_]
                                      [(ftg * 8 + f) * 128:
                                       (ftg * 8 + f + 1) * 128, :],
                                      ch_w2[wg])
                            k.wait("tensor", ch_w2[wg], ch_w2[wg].n)
                            if ftg == 0:
                                k.wait("tensor", ch_yev, ch_yev.n)
                                k.wait("tensor", ch_hev, ch_hev.n)
                                k.wait("tensor", ch_w2[0], ch_w2[0].n)
                            for it in range(4):
                                for dc in range(2):
                                    pb = py[it * 2 + dc]
                                    for f in range(8):
                                        ft = ftg * 8 + f
                                        k.do("tensor", lambda e, _pb=pb,
                                             _wg=wg, _f=f, _ft=ft, _it=it,
                                             _half=half, _dc=dc: e.matmul(
                                                 _pb[:, :],
                                                 hT[:, _ft * 1024 +
                                                    (_half * 4 + _it) * 128:
                                                    _ft * 1024 +
                                                    (_half * 4 + _it + 1) * 128],
                                                 w2s[_wg][_f][:,
                                                              _dc * 512:
                                                              (_dc + 1) * 512],
                                                 start=(_ft == 0),
                                                 stop=False))
                            k.do("tensor", lambda e: e.nop(), inc=ch_yf,
                                 auto=False)
                        for it in range(4):
                            for dc in range(2):
                                pb = py[it * 2 + dc]
                                k.do("tensor", lambda e, _pb=pb, _dc=dc:
                                     e.matmul(_pb[:, :], C.onesb_row[:, :],
                                              b2r[0:1, _dc * 512:
                                                  (_dc + 1) * 512],
                                              start=False, stop=True),
                                     inc=ch_ymm)
                        k.wait("vector", ch_ymm, ch_ymm.n)
                        for it in range(4):
                            for dc in range(2):
                                pb = py[it * 2 + dc]
                                i = half * 4 + it
                                if e_ == 0:
                                    k.do("vector", lambda e, _pb=pb, _i=i,
                                         _dc=dc, _e=e_: e.tensor_scalar(
                                             oacc[_i][:, _dc * 512:
                                                      (_dc + 1) * 512],
                                             _pb[:, :],
                                             mw[:, _i * 8 + _e:_i * 8 + _e + 1],
                                             None, ALU.mult), inc=ch_yev)
                                else:
                                    k.do("vector", lambda e, _pb=pb, _i=i,
                                         _dc=dc, _e=e_:
                                         e.scalar_tensor_tensor(
                                             oacc[_i][:, _dc * 512:
                                                      (_dc + 1) * 512],
                                             _pb[:, :],
                                             mw[:, _i * 8 + _e:_i * 8 + _e + 1],
                                             oacc[_i][:, _dc * 512:
                                                      (_dc + 1) * 512],
                                             ALU.mult, ALU.add), inc=ch_yev)
            for i, ti in enumerate(tiles):
                k.wait("sync", ch_x, ch_x.n)
                k.wait("sync", ch_res, ch_res.n)
                k.wait("sync", ch_ln, ch_ln.n)
                k.dma("sync", sx[:, :], xin[ti * 128:(ti + 1) * 128, :], ch_x)
                k.wait("vector", ch_x, ch_x.n)
                k.wait("vector", ch_yev, ch_yev.n)
                k.do("vector", lambda e, _i=i: e.tensor_tensor(
                    oacc[_i][:, :], oacc[_i][:, :], sx[:, :], ALU.add),
                     inc=ch_res)
                k.wait("sync", ch_res, ch_res.n)
                k.dma("sync", xout[ti * 128:(ti + 1) * 128, :],
                      oacc[_i if False else i][:, :], ch_st)
        k.wait("sync", ch_st, ch_st.n)


def emit_l0ffn(k, C, xin, xout):
    emit_moe(k, C, xin, xout, "moe0", 4096, has_br=False)


def emit_l2ffn(k, C, xin, xout):
    emit_moe(k, C, xin, xout, "moe2", 2048, has_br=True)


def emit_l1attn(k, C, xin, xout):
    NM = [("kre", "wk", "C", 1.0, "bk"), ("kim", "wk", "S", -1.0, "bk"),
          ("vre", "wv", "C", 1.0, "bv"), ("vim", "wv", "S", -1.0, "bv"),
          ("qre", "wq", "C", 1.0, "bq"), ("qim", "wq", "S", -1.0, "bq")]
    with contextlib.ExitStack() as bs:
        bcol = C.sb(bs, [128, 24], BF16, "l1a_bcol")
        mrow = [C.sb(bs, [1, 1024], BF16, f"l1a_mrow{i}") for i in range(6)]
        sx = C.sb(bs, [128, 1024], F32, "l1a_sx")
        xln = C.sb(bs, [128, 1024], BF16, "l1a_xln")
        sq = C.sb(bs, [128, 1024], F32, "l1a_sq")
        sq2 = C.sb(bs, [128, 1024], F32, "l1a_sq2")
        cols = C.sb(bs, [128, 8], F32, "l1a_cols")
        xT = [C.sb(bs, [128, 1024], BF16, f"l1a_xT{c}") for c in range(8)]
        msl = [C.sb(bs, [128, 8192], BF16, "l1a_msl0")]
        msl = [msl[0], msl[0]]
        st4 = {n: C.sb(bs, [128, 8 * 1024], BF16, f"l1a_st_{n}")
               for n in ("kre", "kim", "vre", "vim")}
        qtile = C.sb(bs, [128, 2048], BF16, "l1a_qt")
        merged = C.sb(bs, [128, 2048], BF16, "l1a_merged")
        cumf = C.sb(bs, [128, 2048], F32, "l1a_cumf")
        t1 = C.sb(bs, [128, 1024], F32, "l1a_t1")
        t2 = C.sb(bs, [128, 1024], F32, "l1a_t2")
        qr_ = C.sb(bs, [128, 1024], F32, "l1a_qr")
        qi_ = C.sb(bs, [128, 1024], F32, "l1a_qi")
        selmb = C.sb(bs, [128, 16], BF16, "l1a_selmb")
        gsb = C.sb(bs, [128, 2048], BF16, "l1a_gsb")
        carr = C.sb(bs, [16, 2048], F32, "l1a_carr")
        crowb = C.sb(bs, [1, 2048], BF16, "l1a_crowb")
        xnew = C.sb(bs, [128, 1024], F32, "l1a_xnew")

        mscr = C.dram([6, 1024, 1024], BF16, "l1a_mscr")
        qspill = C.dram([NT, 128, 2048], BF16, "l1a_qspill")
        cumspill = C.dram([NT, 128, 2048], F32, "l1a_cumspill")
        tot = C.dram([16, 2048], F32, "l1a_tot")
        gout = C.dram([128, 2048], F32, "l1a_gout")
        carryscr = C.dram([16, 2048], F32, "l1a_carryscr")

        ch_ld = k.dchan()
        ch_sld = k.dchan()
        ch_pre = k.chan()
        ch_prev = k.chan()
        ch_spl = k.dchan()
        ch_lna, ch_lnv, ch_ln = k.chan(), k.chan(), k.chan()
        ch_tr, ch_trev = k.chan(), k.chan()
        ch_pm, ch_pev = k.chan(), k.chan()
        ch_cc = k.chan()
        ch_ve = k.chan()
        ch_st = k.dchan()

        k.dma("gpsimd", selmb[:, :], C.inp["l1a_selm"][:, :], ch_ld)
        for wi_, wn in enumerate(("wk", "wv", "wq")):
            k.dma("gpsimd", bcol[:, wi_ * 8:(wi_ + 1) * 8],
                  C.inp[f"l1a_b{wn[1]}"][:, :], ch_ld)
        with contextlib.ExitStack() as ps0:
            csb = C.sb(ps0, [128, 8192], BF16, "l1a_csb")
            ssb = C.sb(ps0, [128, 8192], BF16, "l1a_ssb")
            k.dma("gpsimd", csb[:, :].rearrange("p (c f) -> p c f", f=1024),
                  C.inp["dftC"].rearrange("(c p) f -> p c f", p=128), ch_ld)
            k.dma("gpsimd", ssb[:, :].rearrange("p (c f) -> p c f", f=1024),
                  C.inp["dftS"].rearrange("(c p) f -> p c f", p=128), ch_ld)
            wnat = msl[0]
            mev = qtile[:, 0:512]
            pm = C.ps(ps0, [128, 512], F32, "l1a_pm")
            pbr = C.ps(ps0, [1, 512], F32, "l1a_pbr")
            for wi_, wn in enumerate(("wk", "wv", "wq")):
                k.wait("gpsimd", ch_pre, ch_pre.n)
                k.dma("gpsimd", wnat[:, :].rearrange("p (c f) -> p c f",
                                                     f=1024),
                      C.inp[f"l1a_{wn}"].rearrange("(c p) f -> p c f", p=128),
                      ch_ld)
                k.wait("tensor", ch_ld, ch_ld.n)
                for ti_, (tn, sgn) in enumerate((("C", 1.0), ("S", -1.0))):
                    midx = wi_ * 2 + ti_
                    tsb = csb if tn == "C" else ssb
                    for dchunk in range(8):
                        for fh in range(2):
                            k.wait("tensor", ch_prev, ch_prev.n)
                            for o in range(8):
                                k.do("tensor", lambda e, _o=o, _d=dchunk,
                                     _f=fh, _t=tsb: e.matmul(
                                         pm[:, :],
                                         wnat[:, _o * 1024 + _d * 128:
                                              _o * 1024 + (_d + 1) * 128],
                                         _t[:, _o * 1024 + _f * 512:
                                            _o * 1024 + _f * 512 + 512],
                                         start=(_o == 0), stop=(_o == 7)),
                                     inc=(ch_pre if o == 7 else None))
                            k.wait("scalar", ch_pre, ch_pre.n)
                            k.wait("scalar", ch_spl, ch_spl.n)
                            k.do("scalar", lambda e, _s=sgn: e.activation(
                                mev, pm[:, :], AF.Identity,
                                scale=_s), inc=ch_prev)
                            k.wait("sync", ch_prev, ch_prev.n)
                            k.dma("sync", mscr[midx,
                                               dchunk * 128:(dchunk + 1) * 128,
                                               fh * 512:fh * 512 + 512],
                                  mev, ch_spl)
                    for fh in range(2):
                        k.wait("tensor", ch_prev, ch_prev.n)
                        for o in range(8):
                            k.do("tensor", lambda e, _o=o, _f=fh, _t=tsb,
                                 _w=wi_: e.matmul(
                                     pbr[:, :],
                                     bcol[:, _w * 8 + _o:_w * 8 + _o + 1],
                                     _t[:, _o * 1024 + _f * 512:
                                        _o * 1024 + _f * 512 + 512],
                                     start=(_o == 0), stop=(_o == 7)),
                                 inc=(ch_pre if o == 7 else None))
                        k.wait("scalar", ch_pre, ch_pre.n)
                        k.do("scalar", lambda e, _m=midx, _f=fh, _s=sgn:
                             e.activation(mrow[_m][0:1, _f * 512:
                                                   _f * 512 + 512],
                                          pbr[:, :], AF.Identity, scale=_s),
                             inc=ch_prev)
        k.wait("gpsimd", ch_pre, ch_pre.n)
        k.wait("gpsimd", ch_prev, ch_prev.n)
        k.wait("gpsimd", ch_spl, ch_spl.n)
        psA = contextlib.ExitStack()
        ptr = [C.ps(psA, [128, 128], BF16, f"l1a_ptr{i}") for i in range(2)]
        pp = [C.ps(psA, [128, 512], F32, f"l1a_pp{i}") for i in range(2)]
        pcs = [C.ps(psA, [128, 512], F32, f"l1a_pcs{i}") for i in range(4)]
        for oct_ in range(2):
            tiles = list(range(oct_ * 8, oct_ * 8 + 8))
            if True:
                for i, ti in enumerate(tiles):
                    k.wait("sync", ch_ln, ch_ln.n)
                    k.wait("sync", ch_sld, ch_sld.n)
                    k.dma("sync", sx[:, :], xin[ti * 128:(ti + 1) * 128, :],
                          ch_sld)
                    k.wait("vector", ch_sld, ch_sld.n)
                    k.wait("vector", ch_tr, ch_tr.n)
                    ln_tile(k, sx, xln, sq, cols, ch_lna, ch_lnv,
                            ch_done=ch_ln)
                    k.wait("tensor", ch_ln, ch_ln.n)
                    for c in range(8):
                        sl = c % 2
                        k.wait("tensor", ch_trev, ch_trev.n)
                        k.do("tensor", lambda e, _c=c, _sl=sl: e.transpose(
                            ptr[_sl][:, :], xln[:, _c * 128:(_c + 1) * 128],
                            C.identb[:, :]), inc=ch_tr)
                        k.wait("vector", ch_tr, ch_tr.n)
                        k.do("vector", lambda e, _c=c, _sl=sl, _i=i:
                             e.tensor_copy(xT[_c][:, _i * 128:(_i + 1) * 128],
                                           ptr[_sl][:, :]), inc=ch_trev)
                for midx, (dn, _wn, _tn, _sg, _bn) in enumerate(NM):
                    wb = midx % 2
                    gmi = oct_ * 6 + midx
                    if gmi >= 1:
                        k.wait("gpsimd", ch_pm, ch_pm.n)
                    k.dma("gpsimd",
                          msl[wb][:, :].rearrange("p (c f) -> p c f", f=1024),
                          mscr[midx].rearrange("(c p) f -> p c f", p=128),
                          ch_ld)
                    k.wait("tensor", ch_ld, ch_ld.n)
                    k.wait("tensor", ch_trev, ch_trev.n)
                    for i in range(8):
                        for fh in range(2):
                            pb = pp[fh]
                            k.wait("tensor", ch_pev, ch_pev.n - 1 + fh)
                            for c in range(8):
                                k.do("tensor", lambda e, _pb=pb, _wb=wb,
                                     _c=c, _i=i, _f=fh: e.matmul(
                                         _pb[:, :],
                                         xT[_c][:, _i * 128:(_i + 1) * 128],
                                         msl[_wb][:, _c * 1024 + _f * 512:
                                                  _c * 1024 + _f * 512 + 512],
                                         start=(_c == 0), stop=False))
                            k.do("tensor", lambda e, _pb=pb, _m=midx, _f=fh:
                                 e.matmul(_pb[:, :], C.onesb_row[:, :],
                                          mrow[_m][0:1, _f * 512:
                                                   _f * 512 + 512],
                                          start=False, stop=True),
                                 inc=(ch_pm if fh == 1 else None))
                        k.wait("scalar", ch_pm, ch_pm.n)
                        for fh in range(2):
                            if dn.startswith("q"):
                                dst = lambda _f=fh, _m=midx: qtile[
                                    :, (0 if _m == 4 else 1) * 1024 +
                                    _f * 512:(0 if _m == 4 else 1) * 1024 +
                                    _f * 512 + 512]
                            else:
                                dst = lambda _f=fh, _i=i, _d=dn: st4[_d][
                                    :, _i * 1024 + _f * 512:
                                    _i * 1024 + _f * 512 + 512]
                            k.do("scalar", lambda e, _dst=dst, _f=fh:
                                 e.activation(_dst(), pp[_f][:, :],
                                              AF.Identity), inc=ch_pev)
                        if dn == "qim":
                            k.wait("sync", ch_pev, ch_pev.n)
                            k.dma("sync", qspill[tiles[i]], qtile[:, :],
                                  ch_spl)
                            k.wait("scalar", ch_spl, ch_spl.n)
                for i, ti in enumerate(tiles):
                    v = "vector"
                    k.wait(v, ch_pev, ch_pev.n)
                    k.wait(v, ch_ve, ch_ve.n)
                    sl = lambda d, _i=i: st4[d][:, _i * 1024:(_i + 1) * 1024]
                    k.do(v, lambda e, _i=i: e.tensor_tensor(
                        t1[:, :], sl("kre"), sl("vre"), ALU.mult))
                    k.do(v, lambda e, _i=i: e.tensor_tensor(
                        t2[:, :], sl("kim"), sl("vim"), ALU.mult))
                    k.do(v, lambda e: e.tensor_tensor(
                        merged[:, 0:1024], t1[:, :], t2[:, :], ALU.subtract))
                    k.do(v, lambda e, _i=i: e.tensor_tensor(
                        t1[:, :], sl("kre"), sl("vim"), ALU.mult))
                    k.do(v, lambda e, _i=i: e.tensor_tensor(
                        t2[:, :], sl("kim"), sl("vre"), ALU.mult))
                    k.do(v, lambda e: e.tensor_tensor(
                        merged[:, 1024:2048], t1[:, :], t2[:, :], ALU.add),
                         inc=ch_ve)
                    k.wait("tensor", ch_ve, ch_ve.n)
                    k.wait("tensor", ch_cc, ch_cc.n)
                    for q4 in range(4):
                        k.do("tensor", lambda e, _q=q4: e.matmul(
                            pcs[_q][:, :], C.trib[:, :],
                            merged[:, _q * 512:(_q + 1) * 512],
                            start=True, stop=True),
                             inc=(ch_pm if q4 == 3 else None))
                    k.wait("vector", ch_pm, ch_pm.n)
                    k.wait("vector", ch_spl, ch_spl.n)
                    for q4 in range(4):
                        k.do(v, lambda e, _q=q4: e.tensor_copy(
                            cumf[:, _q * 512:(_q + 1) * 512], pcs[_q][:, :]),
                             inc=(ch_cc if q4 == 3 else None))
                    k.wait("sync", ch_cc, ch_cc.n)
                    k.dma("sync", cumspill[ti], cumf[:, :], ch_spl)
                    k.dma("sync", tot[ti], cumf[127:128, :], ch_spl)
        psA.close()
        k.wait("gpsimd", ch_spl, ch_spl.n)
        cc_sem = k.dchan()
        k.do("gpsimd", lambda e: e.collective_compute(
            "AllGather", ALU.bypass,
            replica_groups=[list(range(NCORES))],
            ins=[tot], outs=[gout]), inc16=None, auto=False)
        k.ops["gpsimd"][-1] = (lambda f: lambda e: f(e).then_inc(
            cc_sem.sem, 1))(k.ops["gpsimd"][-1])
        cc_sem.n = 1
        k.ops["gpsimd"].append(lambda e: e.wait_ge(cc_sem.sem, 1))
        k.dma("gpsimd", gsb[:, :], gout[:, :], ch_ld)
        with contextlib.ExitStack() as psC:
            pca = [C.ps(psC, [16, 512], F32, f"l1a_pca{i}") for i in range(4)]
            k.wait("tensor", ch_ld, ch_ld.n)
            for q4 in range(4):
                k.do("tensor", lambda e, _q=q4: e.matmul(
                    pca[_q][:, :], selmb[:, :],
                    gsb[:, _q * 512:(_q + 1) * 512],
                    start=True, stop=True), inc=(ch_pm if q4 == 3 else None))
            k.wait("vector", ch_pm, ch_pm.n)
            for q4 in range(4):
                k.do("vector", lambda e, _q=q4: e.tensor_copy(
                    carr[:, _q * 512:(_q + 1) * 512], pca[_q][:, :]),
                     inc=(ch_ve if q4 == 3 else None))
            k.wait("sync", ch_ve, ch_ve.n)
            k.dma("sync", carryscr[:, :], carr[:, :], ch_spl)
        with contextlib.ExitStack() as psB:
            csb = C.sb(psB, [128, 8192], BF16, "l1a_csb2")
            ssb = C.sb(psB, [128, 8192], BF16, "l1a_ssb2")
            k.dma("gpsimd", csb[:, :].rearrange("p (c f) -> p c f", f=1024),
                  C.inp["dftC"].rearrange("(c p) f -> p c f", p=128), ch_ld)
            k.dma("gpsimd", ssb[:, :].rearrange("p (c f) -> p c f", f=1024),
                  C.inp["dftS"].rearrange("(c p) f -> p c f", p=128), ch_ld)
            pbc = [C.ps(psB, [128, 512], F32, f"l1a_pbc{i}") for i in range(4)]
            pif = [C.ps(psB, [128, 512], F32, f"l1a_pif{i}") for i in range(2)]
            for ti in range(NT):
                k.wait("gpsimd", ch_spl, ch_spl.n)
                k.wait("gpsimd", ch_ve, ch_ve.n)
                k.wait("gpsimd", ch_pm, ch_pm.n)
                k.dma("gpsimd", crowb[:, :], carryscr[ti:ti + 1, :], ch_ld)
                k.wait("sync", ch_ve, ch_ve.n)
                k.dma("sync", cumf[:, :], cumspill[ti], ch_sld)
                k.dma("gpsimd", qtile[:, :], qspill[ti], ch_ld)
                k.wait("tensor", ch_ld, ch_ld.n)
                k.wait("tensor", ch_ve, ch_ve.n)
                for q4 in range(4):
                    k.do("tensor", lambda e, _q=q4: e.matmul(
                        pbc[_q][:, :], C.onesb_row[:, :],
                        crowb[0:1, _q * 512:(_q + 1) * 512],
                        start=True, stop=True),
                         inc=(ch_pm if q4 == 3 else None))
                v = "vector"
                k.wait(v, ch_pm, ch_pm.n)
                k.wait(v, ch_sld, ch_sld.n)
                for q4 in range(4):
                    k.do(v, lambda e, _q=q4: e.tensor_tensor(
                        cumf[:, _q * 512:(_q + 1) * 512],
                        cumf[:, _q * 512:(_q + 1) * 512],
                        pbc[_q][:, :], ALU.add),
                         inc=(ch_ve if q4 == 3 else None))
                k.do(v, lambda e: e.tensor_tensor(
                    t1[:, :], cumf[:, 0:1024], qtile[:, 0:1024], ALU.mult))
                k.do(v, lambda e: e.tensor_tensor(
                    t2[:, :], cumf[:, 1024:2048], qtile[:, 1024:2048],
                    ALU.mult))
                k.do(v, lambda e: e.tensor_tensor(
                    qr_[:, :], t1[:, :], t2[:, :], ALU.add))
                k.do(v, lambda e: e.tensor_tensor(
                    t1[:, :], cumf[:, 1024:2048], qtile[:, 0:1024], ALU.mult))
                k.do(v, lambda e: e.tensor_tensor(
                    t2[:, :], cumf[:, 0:1024], qtile[:, 1024:2048], ALU.mult))
                k.do(v, lambda e: e.tensor_tensor(
                    qi_[:, :], t1[:, :], t2[:, :], ALU.subtract), inc=ch_lnv)
                s1, s2, sc, nsc = (cols[:, 0:1], cols[:, 1:2], cols[:, 2:3],
                                   cols[:, 3:4])
                k.wait("scalar", ch_lnv, ch_lnv.n)
                k.do("scalar", lambda e: e.activation(
                    sq[:, :], qr_[:, :], AF.Square, accum_out=s1))
                k.do("scalar", lambda e: e.activation(
                    sq2[:, :], qi_[:, :], AF.Square, accum_out=s2),
                     inc=ch_lna)
                k.wait(v, ch_lna, ch_lna.n)
                k.do(v, lambda e: e.tensor_tensor(sc, s1, s2, ALU.add),
                     inc=ch_lnv)
                k.wait("scalar", ch_lnv, ch_lnv.n)
                k.do("scalar", lambda e: e.activation(sc, sc, AF.Sqrt),
                     inc=ch_lna)
                k.wait(v, ch_lna, ch_lna.n)
                k.do(v, lambda e: e.tensor_scalar(sc, sc, 1e-9, None,
                                                  ALU.add))
                k.do(v, lambda e: e.reciprocal(nsc, sc))
                k.do(v, lambda e: e.tensor_scalar(sc, nsc, -1.0, None,
                                                  ALU.mult))
                k.do(v, lambda e: e.tensor_scalar(
                    qtile[:, 0:1024], qr_[:, :], nsc, None, ALU.mult))
                k.do(v, lambda e: e.tensor_scalar(
                    qtile[:, 1024:2048], qi_[:, :], sc, None, ALU.mult),
                     inc=ch_ve)
                k.wait("tensor", ch_ve, ch_ve.n)
                k.wait("tensor", ch_pev, ch_pev.n)
                for fh in range(2):
                    pb = pif[fh]
                    for c in range(8):
                        k.do("tensor", lambda e, _pb=pb, _c=c, _f=fh:
                             e.matmul(_pb[:, :],
                                      qtile[:, _c * 128:(_c + 1) * 128],
                                      csb[:, _c * 1024 + _f * 512:
                                          _c * 1024 + _f * 512 + 512],
                                      start=(_c == 0), stop=False))
                    for c in range(8):
                        k.do("tensor", lambda e, _pb=pb, _c=c, _f=fh:
                             e.matmul(_pb[:, :],
                                      qtile[:, 1024 + _c * 128:
                                            1024 + (_c + 1) * 128],
                                      ssb[:, _c * 1024 + _f * 512:
                                          _c * 1024 + _f * 512 + 512],
                                      start=False, stop=(_c == 7)),
                             inc=(ch_pm if (fh == 1 and c == 7) else None))
                k.wait("sync", ch_st, ch_st.n)
                k.wait("sync", ch_ln, ch_ln.n)
                k.dma("sync", sx[:, :], xin[ti * 128:(ti + 1) * 128, :],
                      ch_sld)
                k.wait(v, ch_pm, ch_pm.n)
                k.wait(v, ch_sld, ch_sld.n)
                for fh in range(2):
                    k.do(v, lambda e, _f=fh: e.scalar_tensor_tensor(
                        xnew[:, _f * 512:(_f + 1) * 512], pif[_f][:, :],
                        1.0 / 32768.0, sx[:, _f * 512:(_f + 1) * 512],
                        ALU.mult, ALU.add), inc=(ch_ln if fh == 1 else None))
                k.wait("sync", ch_ln, ch_ln.n)
                k.dma("sync", xout[ti * 128:(ti + 1) * 128, :], xnew[:, :],
                      ch_st)
        k.wait("sync", ch_st, ch_st.n)
        k.wait("sync", ch_spl, ch_spl.n)


def emit_l3attn(k, C, xin, xout):
    NH = 16
    with contextlib.ExitStack() as bs:
        sx = C.sb(bs, [128, 1024], F32, "l3a_sx")
        xln = C.sb(bs, [128, 1024], BF16, "l3a_xln")
        sq = C.sb(bs, [128, 1024], F32, "l3a_sq")
        cols = C.sb(bs, [128, 8], F32, "l3a_cols")
        wsl = C.sb(bs, [128, 8192], BF16, "l3a_wsl")
        qT = C.sb(bs, [128, 16384], BF16, "l3a_qT")
        kT = C.sb(bs, [128, 16384], BF16, "l3a_kT")
        vall = C.sb(bs, [128, 16384], BF16, "l3a_v")
        lmtm = C.sb(bs, [32, 2048], F32, "l3a_lmtm")
        qlmT = C.sb(bs, [64, 4096], BF16, "l3a_qlmT")
        klmT = C.sb(bs, [64, 4096], BF16, "l3a_klmT")
        pT_all = C.sb(bs, [64, 4096], BF16, "l3a_pT")
        pnall = C.sb(bs, [64, 4096], BF16, "l3a_pn")
        i7 = C.sb(bs, [64, 512], F32, "l3a_i7")
        i13 = C.sb(bs, [64, 512], F32, "l3a_i13")
        i15 = C.sb(bs, [64, 512], F32, "l3a_i15")
        k0t = C.sb(bs, [64, 1024], BF16, "l3a_k0t")
        vcur = C.sb(bs, [64, 1024], BF16, "l3a_vcur")
        vcurT = C.sb(bs, [64, 1024], BF16, "l3a_vcurT")
        mT = C.sb(bs, [64, 1024], BF16, "l3a_mT")
        tcur = C.sb(bs, [64, 1024], BF16, "l3a_tcur")
        s64 = C.sb(bs, [64, 1024], F32, "l3a_s64")
        c64 = C.sb(bs, [64, 8], F32, "l3a_c64")
        r64 = C.sb(bs, [1, 8], F32, "l3a_r64")
        rib = C.sb(bs, [1, 8], BF16, "l3a_rib")
        expn = C.sb(bs, [64, 512], BF16, "l3a_expn")
        kst = C.sb(bs, [64, 512], BF16, "l3a_kst")
        qst = C.sb(bs, [64, 128], BF16, "l3a_qst")
        expnT = C.sb(bs, [128, 256], BF16, "l3a_expnT")
        nvp = C.sb(bs, [64, 4160], F32, "l3a_nvp")
        mxl = C.sb(bs, [64, 64], F32, "l3a_mxl")
        nmxg = C.sb(bs, [64, 64], F32, "l3a_nmxg")
        mxs = C.sb(bs, [64, 64], F32, "l3a_mxs")
        expl = C.sb(bs, [128, 64], F32, "l3a_expl")
        explb = C.sb(bs, [128, 64], BF16, "l3a_explb")
        explT = C.sb(bs, [64, 128], BF16, "l3a_explT")
        lcols = C.sb(bs, [128, 4], F32, "l3a_lcols")
        yT = C.sb(bs, [128, 1024], BF16, "l3a_yT")

        lmd = C.dram([32, 2048], F32, "l3a_lmd")
        lmg = C.dram([256, 2048], F32, "l3a_lmg")
        mxpart = C.dram([64, 64], F32, "l3a_mxpart")
        mxgd = C.dram([512, 64], F32, "l3a_mxgd")
        nvd = C.dram([64, 4160], F32, "l3a_nvd")
        nvg = C.dram([512, 4160], F32, "l3a_nvg")

        ch_ld = k.dchan()
        ch_sld = k.dchan()
        ch_spl = k.dchan()
        ch_lna, ch_lnv, ch_ln = k.chan(), k.chan(), k.chan()
        ch_tr, ch_trev = k.chan(), k.chan()
        ch_pm, ch_pev = k.chan(), k.chan()
        ch_ve = k.chan()
        ch_st = k.dchan()
        cc1, cc2, cc3 = k.chan(), k.chan(), k.chan()
        v = "vector"

        k.do(v, lambda e: e.tensor_copy(s64[:, 0:64], C.identf[0:64, 0:64]))
        for t_, v_ in ((i7, 7.0), (i13, 13.0), (i15, 15.0)):
            for rep in range(8):
                k.do(v, lambda e, _t=t_, _v=v_, _r=rep: e.tensor_scalar(
                    _t[:, _r * 64:(_r + 1) * 64], s64[:, 0:64], _v, None,
                    ALU.mult), inc=(ch_ve if rep == 7 else None))

        with contextlib.ExitStack() as psA:
            xTall = C.sb(psA, [128, 1024], BF16, "l3a_xT8")
            ptr = [C.ps(psA, [128, 128], BF16, f"l3a_ptr{i}") for i in range(2)]
            ptf = C.ps(psA, [128, 128], F32, "l3a_ptf")
            pqk = [C.ps(psA, [128, 128], F32, f"l3a_pqk{i}") for i in range(2)]
            pv = [C.ps(psA, [128, 512], F32, f"l3a_pv{i}") for i in range(2)]
            for part in range(3):
                k.wait("gpsimd", ch_pm, ch_pm.n)
                k.dma("gpsimd",
                      wsl[:, :].rearrange("p (c f) -> p c f", f=1024),
                      C.inp["l3a_wattn"].rearrange("(c p) f -> p c f", p=128)
                      [:, :, part * 1024:(part + 1) * 1024], ch_ld)
                k.wait("tensor", ch_ld, ch_ld.n)
                for ti in range(NT):
                    if True:
                        k.wait("sync", ch_ln, ch_ln.n)
                        k.wait("sync", ch_sld, ch_sld.n)
                        k.dma("sync", sx[:, :],
                              xin[ti * 128:(ti + 1) * 128, :], ch_sld)
                        k.wait("vector", ch_sld, ch_sld.n)
                        k.wait("vector", ch_tr, ch_tr.n)
                        ln_tile(k, sx, xln, sq, cols, ch_lna, ch_lnv,
                                ch_done=ch_ln)
                        k.wait("tensor", ch_ln, ch_ln.n)
                        k.wait("tensor", ch_pm, ch_pm.n)
                        for c in range(8):
                            sl = c % 2
                            k.wait("tensor", ch_trev, ch_trev.n)
                            k.do("tensor", lambda e, _c=c, _sl=sl:
                                 e.transpose(ptr[_sl][:, :],
                                             xln[:, _c * 128:(_c + 1) * 128],
                                             C.identb[:, :]), inc=ch_tr)
                            k.wait("vector", ch_tr, ch_tr.n)
                            k.do("vector", lambda e, _c=c, _sl=sl, _ti=ti:
                                 e.tensor_copy(
                                     xTall[:, _c * 128:(_c + 1) * 128],
                                     ptr[_sl][:, :]), inc=ch_trev)
                        k.wait("tensor", ch_trev, ch_trev.n)
                    if part in (0, 1):
                        dst = qT if part == 0 else kT
                        for fc in range(8):
                            pb = pqk[fc % 2]
                            k.wait("tensor", ch_pev, max(0, ch_pev.n - 1))
                            for c in range(8):
                                k.do("tensor", lambda e, _pb=pb, _c=c,
                                     _fc=fc, _ti=ti: e.matmul(
                                         _pb[:, :],
                                         wsl[:, _c * 1024 + _fc * 128:
                                             _c * 1024 + (_fc + 1) * 128],
                                         xTall[:, _c * 128:(_c + 1) * 128],
                                         start=(_c == 0), stop=(_c == 7)),
                                     inc=(ch_pm if c == 7 else None))
                            k.wait("scalar", ch_pm, ch_pm.n)
                            k.do("scalar", lambda e, _pb=pb, _fc=fc,
                                 _ti=ti, _d=dst: e.activation(
                                     _d[:, _fc * 2048 + _ti * 128:
                                        _fc * 2048 + (_ti + 1) * 128],
                                     _pb[:, :], AF.Identity), inc=ch_pev)
                    else:
                        for dc in range(2):
                            pb = pv[dc]
                            k.wait("tensor", ch_pev, max(0, ch_pev.n - 1))
                            for c in range(8):
                                k.do("tensor", lambda e, _pb=pb, _c=c,
                                     _dc=dc, _ti=ti: e.matmul(
                                         _pb[:, :],
                                         xTall[:, _c * 128:(_c + 1) * 128],
                                         wsl[:, _c * 1024 + _dc * 512:
                                             _c * 1024 + _dc * 512 + 512],
                                         start=(_c == 0), stop=(_c == 7)),
                                     inc=(ch_pm if c == 7 else None))
                            k.wait("scalar", ch_pm, ch_pm.n)
                            k.do("scalar", lambda e, _pb=pb, _dc=dc, _ti=ti:
                                 e.activation(
                                     vall[:, _ti * 1024 + _dc * 512:
                                          _ti * 1024 + _dc * 512 + 512],
                                     _pb[:, :], AF.Identity), inc=ch_pev)
            k.wait("vector", ch_pev, ch_pev.n)
            for half, srcT in ((0, qT), (1, kT)):
                for fc in range(8):
                    for ti in range(NT):
                        for s in range(2):
                            k.do(v, lambda e, _s=srcT, _fc=fc, _ti=ti,
                                 _sg=s: e.reduce_sum(
                                     sq[:, _fc * 32 + _ti * 2 + _sg:
                                        _fc * 32 + _ti * 2 + _sg + 1],
                                     _s[:, _fc * 2048 + _ti * 128 +
                                        _sg * 64:_fc * 2048 + _ti * 128 +
                                        _sg * 64 + 64], axis=AX.X))
                k.do(v, lambda e: e.tensor_scalar(
                    sq[:, 0:256], sq[:, 0:256], 1.0 / 64.0, None, ALU.mult),
                     inc=ch_ve)
                k.wait("tensor", ch_ve, ch_ve.n)
                for fc in range(8):
                    k.wait("tensor", ch_trev, ch_trev.n)
                    k.do("tensor", lambda e, _fc=fc: e.transpose(
                        ptf[0:32, 0:128], sq[:, _fc * 32:(_fc + 1) * 32],
                        C.identf[:, :]), inc=ch_tr)
                    k.wait(v, ch_tr, ch_tr.n)
                    k.do(v, lambda e, _h=half, _fc=fc: e.tensor_copy(
                        lmtm[:, _h * 1024 + _fc * 128:
                             _h * 1024 + (_fc + 1) * 128],
                        ptf[0:32, 0:128]), inc=ch_trev)
            k.wait("sync", ch_trev, ch_trev.n)
            k.dma("sync", lmd[:, :], lmtm[:, :], ch_spl)
        k.wait("gpsimd", ch_spl, ch_spl.n)
        k.do("gpsimd", lambda e: e.collective_compute(
            "AllGather", ALU.bypass, replica_groups=[list(range(NCORES))],
            ins=[lmd], outs=[lmg]), inc=cc1, auto=False)

        with contextlib.ExitStack() as psB:
            ptr2 = [C.ps(psB, [128, 128], BF16, f"l3a_p2{i}")
                    for i in range(2)]
            psBC = contextlib.ExitStack()
            pk = [C.ps(psBC, [64, 512], F32, f"l3a_pk{i}") for i in range(2)]
            ps_ = C.ps(psBC, [64, 512], F32, "l3a_psS")
            pnv = C.ps(psBC, [64, 64], F32, "l3a_pnv")
            p1c = C.ps(psBC, [64, 128], F32, "l3a_p1c")
            qlm = sq
            for b in range(B):
                k.wait("gpsimd", cc1, 1)
                k.wait("gpsimd", ch_ve, ch_ve.n)
                for half in range(2):
                    k.wait("gpsimd", ch_ve, ch_ve.n)
                    for r in range(8):
                        k.dma("gpsimd", sq[r * 8:(r + 1) * 8, 0:1024],
                              lmg.rearrange("(r bb js) f -> r bb js f",
                                            bb=4, js=8)
                              [r, b, :, half * 1024:(half + 1) * 1024],
                              ch_ld)
                    k.wait("vector", ch_ld, ch_ld.n)
                    k.do(v, lambda e: e.tensor_copy(
                        xln[0:64, 0:1024], sq[0:64, 0:1024]), inc=ch_ve)
                    k.wait("tensor", ch_ve, ch_ve.n)
                    for h in range(NH):
                        sl = h % 2
                        k.wait("tensor", ch_trev, ch_trev.n)
                        k.do("tensor", lambda e, _h=h, _sl=sl: e.transpose(
                            ptr2[_sl][0:64, 0:64],
                            xln[0:64, _h * 64:(_h + 1) * 64],
                            C.identb[0:64, 0:64]), inc=ch_tr)
                        k.wait(v, ch_tr, ch_tr.n)
                        k.do(v, lambda e, _h=h, _sl=sl, _hf=half, _b=b:
                             e.tensor_copy(
                                 (qlmT if _hf == 0 else klmT)
                                 [:, (_b * NH + _h) * 64:
                                  (_b * NH + _h + 1) * 64],
                                 ptr2[_sl][0:64, 0:64]), inc=ch_trev)
                k.wait("tensor", ch_trev, ch_trev.n)
                for g8 in range(2):
                    pb = pk[g8]
                    k.wait("tensor", ch_ve, ch_ve.n)
                    for hh in range(8):
                        h = g8 * 8 + hh
                        k.do("tensor", lambda e, _pb=pb, _h=h, _hh=hh,
                             _b=b: e.matmul(
                                 _pb[:, _hh * 64:(_hh + 1) * 64],
                                 qlmT[:, (_b * NH + _h) * 64:
                                      (_b * NH + _h + 1) * 64],
                                 klmT[:, (_b * NH + _h) * 64:
                                      (_b * NH + _h + 1) * 64],
                                 start=True, stop=True),
                             inc=(ch_pm if hh == 7 else None))
                    k.wait(v, ch_pm, ch_pm.n)
                    k.do(v, lambda e, _pb=pb, _g=g8: e.tensor_copy(
                        s64[:, _g * 512:(_g + 1) * 512], _pb[:, :]),
                         inc=ch_ve)
                for h in range(NH):
                    mx, nmx, sume, rs = (c64[:, 0:1], c64[:, 1:2],
                                         c64[:, 2:3], c64[:, 3:4])
                    k.do(v, lambda e, _h=h: e.reduce_max(
                        mx, s64[:, _h * 64:(_h + 1) * 64], axis=AX.X))
                    k.do(v, lambda e: e.tensor_scalar(
                        nmx, mx, -1.0, None, ALU.mult), inc=ch_lnv)
                    k.wait("scalar", ch_lnv, ch_lnv.n)
                    k.wait("scalar", ch_ve, ch_ve.n)
                    k.do("scalar", lambda e, _h=h: e.activation(
                        expl[0:64, 0:64], s64[:, _h * 64:(_h + 1) * 64],
                        AF.Exp, bias=nmx, accum_out=sume), inc=ch_lna)
                    k.wait(v, ch_lna, ch_lna.n)
                    k.do(v, lambda e: e.reciprocal(rs, sume))
                    k.do(v, lambda e, _h=h: e.tensor_scalar(
                        vcur[:, _h * 64:(_h + 1) * 64], expl[0:64, 0:64],
                        rs, None, ALU.mult), inc=ch_ve)
                k.wait("tensor", ch_ve, ch_ve.n)
                for h in range(NH):
                    sl = h % 2
                    k.wait("tensor", ch_trev, ch_trev.n)
                    k.do("tensor", lambda e, _h=h, _sl=sl: e.transpose(
                        ptr2[_sl][0:64, 0:64], vcur[:, _h * 64:(_h + 1) * 64],
                        C.identb[0:64, 0:64]), inc=ch_tr)
                    k.wait(v, ch_tr, ch_tr.n)
                    k.do(v, lambda e, _h=h, _sl=sl: e.tensor_copy(
                        k0t[:, _h * 64:(_h + 1) * 64],
                        ptr2[_sl][0:64, 0:64]), inc=ch_trev)
                k.wait("tensor", ch_trev, ch_trev.n)
                for h in range(NH):
                    k.do("tensor", lambda e, _h=h: e.matmul(
                        p1c[0:1, 0:64], C.onesb_col[0:64, 0:1],
                        vcur[:, _h * 64:(_h + 1) * 64],
                        start=True, stop=True), inc=ch_pm)
                    k.wait(v, ch_pm, ch_pm.n)
                    k.do(v, lambda e: e.reduce_max(r64[0:1, 0:1],
                                                   p1c[0:1, 0:64], axis=AX.X))
                    k.do(v, lambda e: e.reciprocal(r64[0:1, 1:2],
                                                   r64[0:1, 0:1]))
                    k.do(v, lambda e: e.tensor_copy(rib[0:1, 0:1],
                                                    r64[0:1, 1:2]),
                         inc=ch_ve)
                    k.wait("tensor", ch_ve, ch_ve.n)
                    k.do("tensor", lambda e: e.matmul(
                        p1c[0:64, 64:65], C.onesb_row[0:1, 0:64],
                        rib[0:1, 0:1], start=True, stop=True), inc=ch_pm)
                    k.wait(v, ch_pm, ch_pm.n)
                    k.do(v, lambda e: e.tensor_copy(c64[:, 4:5],
                                                    p1c[0:64, 64:65]))
                    k.do(v, lambda e, _h=h: e.tensor_scalar(
                        vcurT[:, _h * 64:(_h + 1) * 64],
                        vcur[:, _h * 64:(_h + 1) * 64],
                        c64[0:64, 4:5], None, ALU.mult))
                    k.do(v, lambda e, _h=h: e.tensor_scalar(
                        vcur[:, _h * 64:(_h + 1) * 64],
                        k0t[:, _h * 64:(_h + 1) * 64],
                        c64[0:64, 4:5], None, ALU.mult), inc=ch_ve)
                for it in range(6):
                    for g8 in range(2):
                        k.wait("tensor", ch_ve, ch_ve.n)
                        for hh in range(8):
                            h = g8 * 8 + hh
                            k.do("tensor", lambda e, _h=h, _hh=hh: e.matmul(
                                pk[0][:, _hh * 64:(_hh + 1) * 64],
                                k0t[:, _h * 64:(_h + 1) * 64],
                                vcur[:, _h * 64:(_h + 1) * 64],
                                start=True, stop=True))
                            k.do("tensor", lambda e, _h=h, _hh=hh: e.matmul(
                                pk[1][:, _hh * 64:(_hh + 1) * 64],
                                vcur[:, _h * 64:(_h + 1) * 64],
                                k0t[:, _h * 64:(_h + 1) * 64],
                                start=True, stop=True),
                                 inc=(ch_pm if hh == 7 else None))
                        k.wait(v, ch_pm, ch_pm.n)
                        k.do(v, lambda e, _g=g8: e.scalar_tensor_tensor(
                            tcur[:, _g * 512:(_g + 1) * 512], pk[0][:, :],
                            -1.0, i7[:, :], ALU.mult, ALU.add))
                        k.do(v, lambda e, _g=g8: e.tensor_copy(
                            mT[:, _g * 512:(_g + 1) * 512], pk[1][:, :]),
                             inc=ch_ve)
                    for const_, nm in ((i15, "z"), (i13, "u")):
                        for g8 in range(2):
                            k.wait("tensor", ch_ve, ch_ve.n)
                            for hh in range(8):
                                h = g8 * 8 + hh
                                k.do("tensor", lambda e, _h=h, _hh=hh:
                                     e.matmul(
                                         pk[0][:, _hh * 64:(_hh + 1) * 64],
                                         mT[:, _h * 64:(_h + 1) * 64],
                                         tcur[:, _h * 64:(_h + 1) * 64],
                                         start=True, stop=True),
                                     inc=(ch_pm if hh == 7 else None))
                            k.wait(v, ch_pm, ch_pm.n)
                            k.do(v, lambda e, _g=g8, _c=const_:
                                 e.scalar_tensor_tensor(
                                     tcur[:, _g * 512:(_g + 1) * 512],
                                     pk[0][:, :], -1.0, _c[:, :],
                                     ALU.mult, ALU.add), inc=ch_ve)
                    for g8 in range(2):
                        k.wait("tensor", ch_ve, ch_ve.n)
                        for hh in range(8):
                            h = g8 * 8 + hh
                            k.do("tensor", lambda e, _h=h, _hh=hh: e.matmul(
                                pk[0][:, _hh * 64:(_hh + 1) * 64],
                                vcurT[:, _h * 64:(_h + 1) * 64],
                                tcur[:, _h * 64:(_h + 1) * 64],
                                start=True, stop=True),
                                 inc=(ch_pm if hh == 7 else None))
                        k.wait("scalar", ch_pm, ch_pm.n)
                        k.wait("scalar", ch_ve, ch_ve.n)
                        k.do("scalar", lambda e, _g=g8: e.activation(
                            vcur[:, _g * 512:(_g + 1) * 512], pk[0][:, :],
                            AF.Identity, scale=0.25), inc=ch_ve)
                    k.wait("tensor", ch_ve, ch_ve.n)
                    for h in range(NH):
                        sl = h % 2
                        k.wait("tensor", ch_trev, ch_trev.n)
                        k.do("tensor", lambda e, _h=h, _sl=sl: e.transpose(
                            ptr2[_sl][0:64, 0:64],
                            vcur[:, _h * 64:(_h + 1) * 64],
                            C.identb[0:64, 0:64]), inc=ch_tr)
                        k.wait(v, ch_tr, ch_tr.n)
                        k.do(v, lambda e, _h=h, _sl=sl: e.tensor_copy(
                            vcurT[:, _h * 64:(_h + 1) * 64],
                            ptr2[_sl][0:64, 0:64]), inc=ch_trev)
                    k.wait("tensor", ch_trev, ch_trev.n)
                k.do(v, lambda e, _b=b: e.tensor_copy(
                    pT_all[:, _b * 1024:(_b + 1) * 1024], vcurT[:, :]),
                     inc=ch_ve)
                for h in range(NH):
                    k.wait("vector", ch_pm, ch_pm.n)
                    k.do("vector", lambda e, _h=h, _b=b: e.tensor_copy(
                        kst[:, :],
                        kT[64 * (_h % 2):64 * (_h % 2) + 64,
                           (_h // 2) * 2048 + _b * 512:
                           (_h // 2) * 2048 + _b * 512 + 512]), inc=ch_ve)
                    k.wait("tensor", ch_ve, ch_ve.n)
                    k.do("tensor", lambda e, _h=h, _b=b: e.matmul(
                        ps_[:, :],
                        qlmT[:, (_b * NH + _h) * 64:(_b * NH + _h + 1) * 64],
                        kst[:, :], start=True, stop=True), inc=ch_pm)
                    k.wait(v, ch_pm, ch_pm.n)
                    k.do(v, lambda e, _h=h, _b=b: e.reduce_max(
                        mxl[:, _b * NH + _h:_b * NH + _h + 1], ps_[:, :],
                        axis=AX.X), inc=ch_ve)
            k.wait("sync", ch_ve, ch_ve.n)
            k.dma("sync", mxpart[:, :], mxl[:, :], ch_spl)
            k.wait("gpsimd", ch_spl, ch_spl.n)
            k.do("gpsimd", lambda e: e.collective_compute(
                "AllGather", ALU.bypass,
                replica_groups=[list(range(NCORES))],
                ins=[mxpart], outs=[mxgd]), inc=cc2, auto=False)
            k.wait("gpsimd", cc2, 1)
            k.dma("gpsimd", nmxg[:, :], mxgd[0:64, :], ch_ld)
            for r in range(1, 8):
                k.wait("gpsimd", ch_ve, ch_ve.n)
                k.dma("gpsimd", mxs[:, :], mxgd[r * 64:(r + 1) * 64, :],
                      ch_ld)
                k.wait(v, ch_ld, ch_ld.n)
                k.do(v, lambda e: e.tensor_tensor(
                    nmxg[:, :], nmxg[:, :], mxs[:, :], ALU.max), inc=ch_ve)
            k.wait(v, ch_ld, ch_ld.n)
            k.do(v, lambda e: e.tensor_scalar(
                nmxg[:, :], nmxg[:, :], -1.0, None, ALU.mult), inc=ch_ve)
            for b in range(B):
                for h in range(NH):
                    bh = b * NH + h
                    k.wait("vector", ch_pm, ch_pm.n)
                    k.do("vector", lambda e, _h=h, _b=b: e.tensor_copy(
                        kst[:, :],
                        kT[64 * (_h % 2):64 * (_h % 2) + 64,
                           (_h // 2) * 2048 + _b * 512:
                           (_h // 2) * 2048 + _b * 512 + 512]), inc=ch_ve)
                    k.wait("tensor", ch_ve, ch_ve.n)
                    k.do("tensor", lambda e, _h=h, _b=b: e.matmul(
                        ps_[:, :],
                        qlmT[:, (_b * NH + _h) * 64:(_b * NH + _h + 1) * 64],
                        kst[:, :], start=True, stop=True), inc=ch_pm)
                    k.wait("scalar", ch_pm, ch_pm.n)
                    k.wait("scalar", ch_trev, ch_trev.n)
                    k.do("scalar", lambda e, _bh=bh: e.activation(
                        expn[:, :], ps_[:, :], AF.Exp,
                        bias=nmxg[:, _bh:_bh + 1],
                        accum_out=nvp[:, _bh * 65 + 64:_bh * 65 + 65]),
                         inc=ch_lna)
                    k.wait("tensor", ch_lna, ch_lna.n)
                    for q4 in range(4):
                        sl = q4 % 2
                        k.wait("tensor", ch_trev, ch_trev.n)
                        k.do("tensor", lambda e, _q=q4, _sl=sl: e.transpose(
                            ptr2[_sl][0:128, 0:64],
                            expn[:, _q * 128:(_q + 1) * 128],
                            C.identb[0:64, 0:64]), inc=ch_tr)
                        k.wait(v, ch_tr, ch_tr.n)
                        k.do(v, lambda e, _q=q4, _sl=sl: e.tensor_copy(
                            expnT[:, _q * 64:(_q + 1) * 64],
                            ptr2[_sl][:, 0:64]), inc=ch_trev)
                    k.wait("tensor", ch_trev, ch_trev.n)
                    for q4 in range(4):
                        k.do("tensor", lambda e, _q=q4, _h=h, _b=b:
                             e.matmul(pnv[:, :],
                                      expnT[:, _q * 64:(_q + 1) * 64],
                                      vall[:, (_b * 4 + _q) * 1024 +
                                           _h * 64:(_b * 4 + _q) * 1024 +
                                           (_h + 1) * 64],
                                      start=(_q == 0), stop=(_q == 3)),
                             inc=(ch_pm if q4 == 3 else None))
                    k.wait(v, ch_pm, ch_pm.n)
                    k.do(v, lambda e, _bh=bh: e.tensor_copy(
                        nvp[:, _bh * 65:_bh * 65 + 64], pnv[:, :]),
                         inc=ch_ve)
            k.wait("sync", ch_ve, ch_ve.n)
            k.wait("sync", ch_lna, ch_lna.n)
            k.dma("sync", nvd[:, :], nvp[:, :], ch_spl)
            k.wait("gpsimd", ch_spl, ch_spl.n)
            k.do("gpsimd", lambda e: e.collective_compute(
                "AllGather", ALU.bypass,
                replica_groups=[list(range(NCORES))],
                ins=[nvd], outs=[nvg]), inc=cc3, auto=False)
            k.wait("gpsimd", cc3, 1)
            k.wait("gpsimd", ch_ve, ch_ve.n)
            nvs = sq[0:64, 0:520]
            k.dma("gpsimd", nvp[:, :], nvg[0:64, :], ch_ld)
            for r in range(1, 8):
                for q in range(8):
                    k.wait("gpsimd", ch_ve, ch_ve.n)
                    k.dma("gpsimd", nvs,
                          nvg[r * 64:(r + 1) * 64, q * 520:(q + 1) * 520],
                          ch_ld)
                    k.wait(v, ch_ld, ch_ld.n)
                    k.do(v, lambda e, _q=q, _s=nvs: e.tensor_tensor(
                        nvp[:, _q * 520:(_q + 1) * 520],
                        nvp[:, _q * 520:(_q + 1) * 520], _s,
                        ALU.add), inc=ch_ve)
            k.wait(v, ch_ld, ch_ld.n)
            for bh in range(B * NH):
                k.wait("vector", ch_pm, ch_pm.n)
                k.do(v, lambda e, _bh=bh: e.reciprocal(
                    c64[:, 5:6], nvp[:, _bh * 65 + 64:_bh * 65 + 65]))
                k.do(v, lambda e, _bh=bh: e.tensor_scalar(
                    tcur[:, 0:64], nvp[:, _bh * 65:_bh * 65 + 64],
                    c64[0:64, 5:6], None, ALU.mult), inc=ch_ve)
                k.wait("tensor", ch_ve, ch_ve.n)
                k.wait("tensor", ch_pev, ch_pev.n)
                k.do("tensor", lambda e, _bh=bh: e.matmul(
                    pnv[:, :], pT_all[:, _bh * 64:(_bh + 1) * 64],
                    tcur[:, 0:64], start=True, stop=True), inc=ch_pm)
                k.wait("scalar", ch_pm, ch_pm.n)
                k.do("scalar", lambda e, _bh=bh: e.activation(
                    pnall[:, _bh * 64:(_bh + 1) * 64], pnv[:, :],
                    AF.Identity), inc=ch_pev)
            wpj2 = mT
            k.wait("gpsimd", ch_pev, ch_pev.n)
            k.dma("gpsimd", wsl[:, :].rearrange("p (c f) -> p c f", f=1024),
                  C.inp["l3a_wproj"].rearrange("(c p) f -> p c f", p=128),
                  ch_ld)
            k.wait("tensor", ch_ld, ch_ld.n)
            psBC.close()
            ptf_d = C.ps(psB, [128, 64], F32, "l3a_ptfd")
            pyt_d = C.ps(psB, [64, 128], F32, "l3a_pytd")
            pprj_d = [C.ps(psB, [128, 512], F32, f"l3a_pprj{i}")
                      for i in range(2)]
            for ti in range(NT):
                b = ti // 4
                for h in range(NH):
                    bh = b * NH + h
                    k.wait("vector", ch_pm, ch_pm.n)
                    k.do("vector", lambda e, _h=h, _ti=ti: e.tensor_copy(
                        qst[:, :],
                        qT[64 * (_h % 2):64 * (_h % 2) + 64,
                           (_h // 2) * 2048 + _ti * 128:
                           (_h // 2) * 2048 + (_ti + 1) * 128]), inc=ch_ve)
                    k.wait("tensor", ch_ve, ch_ve.n)
                    k.do("tensor", lambda e, _h=h, _ti=ti: e.matmul(
                        ptf_d[:, 0:64], qst[:, :],
                        klmT[:, (_ti // 4 * NH + _h) * 64:
                             (_ti // 4 * NH + _h + 1) * 64],
                        start=True, stop=True), inc=ch_pm)
                    mx2, nmx2, sume2, rs2 = (lcols[:, 0:1], lcols[:, 1:2],
                                             lcols[:, 2:3], lcols[:, 3:4])
                    k.wait(v, ch_pm, ch_pm.n)
                    k.do(v, lambda e: e.reduce_max(mx2, ptf_d[:, 0:64],
                                                   axis=AX.X))
                    k.do(v, lambda e: e.tensor_scalar(
                        nmx2, mx2, -1.0, None, ALU.mult), inc=ch_lnv)
                    k.wait("scalar", ch_lnv, ch_lnv.n)
                    k.do("scalar", lambda e: e.activation(
                        expl[:, :], ptf_d[:, 0:64], AF.Exp, bias=nmx2,
                        accum_out=sume2), inc=ch_lna)
                    k.wait(v, ch_lna, ch_lna.n)
                    k.do(v, lambda e: e.reciprocal(rs2, sume2))
                    k.do(v, lambda e: e.tensor_scalar(
                        explb[:, :], expl[:, :], rs2, None, ALU.mult),
                         inc=ch_ve)
                    k.wait("tensor", ch_ve, ch_ve.n)
                    k.wait("tensor", ch_trev, ch_trev.n)
                    k.do("tensor", lambda e: e.transpose(
                        ptr2[0][0:64, 0:128], explb[:, :], C.identb[:, :]),
                         inc=ch_tr)
                    k.wait(v, ch_tr, ch_tr.n)
                    k.do(v, lambda e: e.tensor_copy(
                        explT[:, :], ptr2[0][0:64, :]), inc=ch_trev)
                    k.wait("tensor", ch_trev, ch_trev.n)
                    k.do("tensor", lambda e, _bh=bh: e.matmul(
                        ptf_d[:, 64:192] if False else pyt_d[:, :],
                        pnall[:, _bh * 64:(_bh + 1) * 64], explT[:, :],
                        start=True, stop=True), inc=ch_pm)
                    k.wait("scalar", ch_pm, ch_pm.n)
                    k.wait("scalar", ch_ve, ch_ve.n)
                    k.do("scalar", lambda e, _h=h: e.activation(
                        yT[64 * (_h % 2):64 * (_h % 2) + 64,
                           (_h // 2) * 128:(_h // 2) * 128 + 128],
                        pyt_d[:, :], AF.Identity), inc=ch_pev)
                k.wait("tensor", ch_pev, ch_pev.n)
                k.wait("tensor", ch_ve, ch_ve.n)
                for dc in range(2):
                    for c in range(8):
                        k.do("tensor", lambda e, _c=c, _dc=dc: e.matmul(
                            pprj_d[_dc][:, :],
                            yT[:, _c * 128:(_c + 1) * 128],
                            wsl[:, _c * 1024 + _dc * 512:
                                _c * 1024 + _dc * 512 + 512],
                            start=(_c == 0), stop=(_c == 7)),
                             inc=(ch_pm if c == 7 else None))
                k.wait("sync", ch_st, ch_st.n)
                k.wait("sync", ch_ln, ch_ln.n)
                k.dma("sync", sx[:, :], xin[ti * 128:(ti + 1) * 128, :],
                      ch_sld)
                k.wait(v, ch_pm, ch_pm.n)
                k.wait(v, ch_sld, ch_sld.n)
                for dc in range(2):
                    k.do(v, lambda e, _dc=dc: e.tensor_tensor(
                        sq[:, _dc * 512:(_dc + 1) * 512],
                        pprj_d[_dc][:, :],
                        sx[:, _dc * 512:(_dc + 1) * 512], ALU.add),
                         inc=(ch_ln if dc == 1 else None))
                k.wait("sync", ch_ln, ch_ln.n)
                k.dma("sync", xout[ti * 128:(ti + 1) * 128, :], sq[:, :],
                      ch_st)
        k.wait("sync", ch_st, ch_st.n)


def emit_l0attn(k, C, xin, xout):
    with contextlib.ExitStack() as bs:
        v, a, g = "vector", "scalar", "gpsimd"
        sx = C.sb(bs, [128, 1024], F32, "l0_sx")
        xln = C.sb(bs, [128, 1024], BF16, "l0_xln")
        sq = C.sb(bs, [128, 1024], F32, "l0_sq")
        cols = C.sb(bs, [128, 8], F32, "l0_cols")
        carryT = C.sb(bs, [128, 65 * 16], F32, "l0_carryT")
        thbc = C.sb(bs, [128, 4096], F32, "l0_thbc")
        wlcol = C.sb(bs, [128, 3], F32, "l0_wlcol")
        nspb = C.sb(bs, [128, 32], F32, "l0_nspb")
        scb = C.sb(bs, [128, 32], F32, "l0_scb")
        ccol = C.sb(bs, [128, 4], F32, "l0_ccol")
        onesf_row = C.sb(bs, [1, 128], F32, "l0_onesf")
        wtap = C.sb(bs, [128, 68], F32, "l0_wtap")
        cbc = C.sb(bs, [128, 17], F32, "l0_cbc")
        poscs = C.sb(bs, [128, 16], F32, "l0_posc")
        selmb = C.sb(bs, [128, 16], BF16, "l0_selmb")
        zbT = C.sb(bs, [128, 17 * 48], F32, "l0_zbT")
        xbT8 = C.sb(bs, [128, 8 * 48], BF16, "l0_xbT8")

        spn = C.dram([16, 128, 8192], BF16, "l0_spn")
        spd = C.dram([16, 32, 128], F32, "l0_spd")
        spg = C.dram([16, 128, 1024], BF16, "l0_spg")
        agin = C.dram([16, 8224], F32, "l0_agin")
        agout = C.dram([128, 8224], F32, "l0_agout")
        bin_ = C.dram([48, 1024], F32, "l0_bin")
        bga = C.dram([384, 1024], F32, "l0_bga")

        ch_ld = k.dchan()
        ch_sld = k.dchan()
        ch_spn = k.dchan()
        ch_spg = k.dchan()
        ch_tot = k.dchan()
        ch_st = k.dchan()
        ch_lna, ch_lnv, ch_ln = k.chan(), k.chan(), k.chan()
        ch_tr, ch_trev = k.chan(), k.chan()
        ch_pm, ch_pev = k.chan(), k.chan()
        ch_ve, ch_po, ch_se = k.chan(), k.chan(), k.chan()
        cc1 = k.chan()

        k.do(g, lambda e: e.memset(ccol[:, 0:1], 1.5707963267948966))
        k.do(g, lambda e: e.memset(ccol[:, 1:2], 1e-5))
        k.do(g, lambda e: e.memset(onesf_row[:, :], 1.0), inc=ch_po)
        k.dma(g, wtap[:, :], C.inp["l0a_wtap"][:, :], ch_ld)
        k.dma(g, cbc[:, :], C.inp["l0a_cb"][:, :], ch_ld)
        k.dma(g, poscs[:, :], C.inp["l0a_posc"][:, :], ch_ld)
        k.dma(g, selmb[:, :], C.inp["l1a_selm"][:, :], ch_ld)

        for t in range(NT):
            k.dma("sync", bin_[t * 3:(t + 1) * 3, :],
                  xin[t * 128 + 125:t * 128 + 128, :], ch_tot)
        k.wait(g, ch_tot, ch_tot.n)
        k.do(g, lambda e: e.collective_compute(
            "AllGather", ALU.bypass, replica_groups=[list(range(NCORES))],
            ins=[bin_], outs=[bga]), inc=cc1, auto=False)

        with contextlib.ExitStack() as ps0:
            rows = C.sb(ps0, [1, 96], F32, "l0_rows")
            ths = C.sb(ps0, [1, 4096], F32, "l0_ths")
            selxb = C.sb(ps0, [128, 3 * 48], BF16, "l0_selx")
            bgs = [C.sb(ps0, [128, 1024], BF16, f"l0_bgs{i}")
                   for i in range(3)]
            xbnd = C.sb(ps0, [48, 1024], F32, "l0_xbnd")
            xbln = C.sb(ps0, [48, 1024], BF16, "l0_xbln")
            pth = [C.ps(ps0, [128, 512], F32, f"l0_pth{i}") for i in range(2)]
            pbx = [C.ps(ps0, [48, 512], F32, f"l0_pbx{i}") for i in range(2)]
            psm = C.ps(ps0, [128, 48], BF16, "l0_psm")
            dlg, srow = rows[0:1, 0:3], rows[0:1, 8:40]
            k.dma(g, rows[0:1, 0:3], C.inp["l0a_dlg"][:, :], ch_ld)
            k.dma(g, rows[0:1, 8:36], C.inp["l0a_dsl"][:, :], ch_ld)
            k.dma(g, rows[0:1, 36:40], C.inp["l0a_asl"][:, :], ch_ld)
            k.dma(g, rows[0:1, 48:80], C.inp["l0a_ssc"][:, :], ch_ld)
            k.dma(g, ths[:, :], C.inp["l0a_ths"][:, :], ch_ld)
            k.dma(g, selxb[:, :].rearrange("p (c f) -> p c f", f=48),
                  C.inp["l0a_selx"][:, :, :], ch_ld)
            k.wait(v, ch_ld, ch_ld.n)
            mx, nmx = rows[0:1, 4:5], rows[0:1, 5:6]
            sme, rsu = rows[0:1, 6:7], rows[0:1, 7:8]
            k.do(v, lambda e: e.reduce_max(mx, dlg, axis=AX.X))
            k.do(v, lambda e: e.tensor_scalar(nmx, mx, -1.0, None, ALU.mult),
                 inc=ch_lnv)
            k.wait(a, ch_lnv, ch_lnv.n)
            k.do(a, lambda e: e.activation(dlg, dlg, AF.Exp, bias=nmx,
                                           accum_out=sme), inc=ch_lna)
            k.wait(v, ch_lna, ch_lna.n)
            k.do(v, lambda e: e.reciprocal(rsu, sme))
            k.do(v, lambda e: e.tensor_scalar(dlg, dlg, rsu, None, ALU.mult))
            k.do(v, lambda e: e.nop(), inc=ch_ve)
            k.wait(a, ch_ve, ch_ve.n)
            k.do(a, lambda e: e.activation(sq[0:1, 0:32], srow, AF.Abs),
                 inc=ch_se)
            k.wait(a, ch_se, ch_se.n)
            k.do(a, lambda e: e.activation(sq[0:1, 32:64], sq[0:1, 0:32],
                                           AF.Exp, scale=-1.0), inc=ch_lna)
            k.wait(v, ch_lna, ch_lna.n)
            k.do(v, lambda e: e.tensor_scalar(sq[0:1, 32:64], sq[0:1, 32:64],
                                              1.0, None, ALU.add), inc=ch_lnv)
            k.wait(a, ch_lnv, ch_lnv.n)
            k.do(a, lambda e: e.activation(sq[0:1, 32:64], sq[0:1, 32:64],
                                           AF.Ln), inc=ch_lna)
            k.wait(v, ch_lna, ch_lna.n)
            k.do(v, lambda e: e.tensor_scalar(sq[0:1, 64:96], srow, 0.0,
                                              None, ALU.max))
            k.do(v, lambda e: e.tensor_tensor(sq[0:1, 32:64], sq[0:1, 32:64],
                                              sq[0:1, 64:96], ALU.add))
            k.do(v, lambda e: e.tensor_scalar(srow, sq[0:1, 32:64], -1.0,
                                              None, ALU.mult))
            k.do(v, lambda e: e.nop(), inc=ch_ve)
            k.wait("tensor", ch_ve, ch_ve.n)
            k.wait("tensor", ch_ld, ch_ld.n)
            k.wait("tensor", ch_po, ch_po.n)
            k.do("tensor", lambda e: e.matmul(
                pth[0][:, 0:3], onesf_row[:, :], dlg, start=True, stop=True))
            k.do("tensor", lambda e: e.matmul(
                pth[0][:, 8:40], onesf_row[:, :], srow, start=True,
                stop=True))
            k.do("tensor", lambda e: e.matmul(
                pth[0][:, 48:80], onesf_row[:, :], rows[0:1, 48:80],
                start=True, stop=True), inc=ch_pm)
            k.wait(a, ch_pm, ch_pm.n)
            k.do(a, lambda e: e.activation(wlcol[:, :], pth[0][:, 0:3],
                                           AF.Identity))
            k.do(a, lambda e: e.activation(nspb[:, :], pth[0][:, 8:40],
                                           AF.Identity))
            k.do(a, lambda e: e.activation(scb[:, :], pth[0][:, 48:80],
                                           AF.Identity), inc=ch_pev)
            for q in range(8):
                k.wait("tensor", ch_pev, ch_pev.n)
                k.do("tensor", lambda e, _q=q: e.matmul(
                    pth[_q % 2][:, :], onesf_row[:, :],
                    ths[0:1, _q * 512:(_q + 1) * 512], start=True,
                    stop=True), inc=ch_pm)
                k.wait(a, ch_pm, ch_pm.n)
                k.do(a, lambda e, _q=q: e.activation(
                    thbc[:, _q * 512:(_q + 1) * 512], pth[_q % 2][:, :],
                    AF.Identity), inc=ch_pev)
            k.wait(g, cc1, 1)
            for rc in range(3):
                k.dma(g, bgs[rc][:, :], bga[rc * 128:(rc + 1) * 128, :],
                      ch_ld)
            k.wait("tensor", ch_ld, ch_ld.n)
            for half in range(2):
                for rc in range(3):
                    k.do("tensor", lambda e, _r=rc, _h=half: e.matmul(
                        pbx[_h][:, :], selxb[:, _r * 48:(_r + 1) * 48],
                        bgs[_r][:, _h * 512:(_h + 1) * 512],
                        start=(_r == 0), stop=(_r == 2)),
                         inc=(ch_pm if (half == 1 and rc == 2) else None))
            k.wait(a, ch_pm, ch_pm.n)
            for half in range(2):
                k.do(a, lambda e, _h=half: e.activation(
                    xbnd[:, _h * 512:(_h + 1) * 512], pbx[_h][:, :],
                    AF.Identity), inc=(ch_pev if half == 1 else None))
            k.wait(v, ch_pev, ch_pev.n)
            ln_tile(k, xbnd, xbln, sq[0:48, :], cols[0:48, :], ch_lna,
                    ch_lnv, ch_done=ch_ln)
            k.wait("tensor", ch_ln, ch_ln.n)
            for dc in range(8):
                k.do("tensor", lambda e, _d=dc: e.transpose(
                    psm[:, :], xbln[:, _d * 128:(_d + 1) * 128],
                    C.identb[0:48, 0:48]), inc=ch_tr)
                k.wait(v, ch_tr, ch_tr.n)
                k.do(v, lambda e, _d=dc: e.tensor_copy(
                    xbT8[:, _d * 48:(_d + 1) * 48], psm[:, :]), inc=ch_trev)
                k.wait("tensor", ch_trev, ch_trev.n)
            k.barrier()

        with contextlib.ExitStack() as ps1:
            winsb = C.sb(ps1, [128, 8 * 2080], BF16, "l0_win")
            zT = C.sb(ps1, [128, 2176], F32, "l0_zT")
            zc = C.sb(ps1, [128, 2176], F32, "l0_zc")
            xT8 = C.sb(ps1, [128, 1024], BF16, "l0_xT8")
            xval = C.sb(ps1, [128, 1024], F32, "l0_xval")
            gateb = C.sb(ps1, [128, 1024], BF16, "l0_gateb")
            sraw = C.sb(ps1, [128, 32], F32, "l0_sraw")
            mg = C.sb(ps1, [128, 8192], BF16, "l0_mg")
            numbT = C.sb(ps1, [128, 8192], BF16, "l0_numbT")
            denT = C.sb(ps1, [32, 128], F32, "l0_denT")
            pwx = C.sb(ps1, [128, 1024], F32, "l0_pwx")
            ppw = C.sb(ps1, [128, 1024], F32, "l0_ppw")
            t1b = C.sb(ps1, [128, 1024], F32, "l0_t1")
            t2b = C.sb(ps1, [128, 1024], F32, "l0_t2")
            phib = C.sb(ps1, [128, 1024], F32, "l0_phi")
            sinb = C.sb(ps1, [128, 1024], F32, "l0_sin")
            cosb = C.sb(ps1, [128, 1024], F32, "l0_cos")
            smc = C.sb(ps1, [128, 36], F32, "l0_smc")
            pwb = C.sb(ps1, [128, 32], BF16, "l0_pwb")
            pwTb = C.sb(ps1, [32, 128], BF16, "l0_pwT")
            repl = C.sb(ps1, [32, 1024], BF16, "l0_repl")
            totr = C.sb(ps1, [1, 512], F32, "l0_totr")
            pz = [C.ps(ps1, [128, 128], F32, f"l0_pz{i}") for i in range(2)]
            ptr = [C.ps(ps1, [128, 128], F32, f"l0_ptr{i}")
                   for i in range(2)]
            ptrb = [C.ps(ps1, [128, 128], BF16, f"l0_ptrb{i}")
                    for i in range(2)]
            pcu = [C.ps(ps1, [128, 512], F32, f"l0_pcu{i}") for i in range(2)]
            ptb = ptrb[0][0:32, 0:128]
            ptot = pcu[0][0:1, 0:512]
            k.dma(g, winsb[:, :].rearrange("p (c f) -> p c f", f=2080),
                  C.inp["l0a_win"].rearrange("(c p) f -> p c f", p=128),
                  ch_ld)
            k.dma(g, repl[:, :], C.inp["repl32"][:, :], ch_ld)
            k.wait("tensor", ch_ld, ch_ld.n)
            for cc in range(17):
                ncc = 128 if cc < 16 else 32
                pb = pz[cc % 2]
                k.wait("tensor", ch_pev, max(0, ch_pev.n - 1))
                for dc in range(8):
                    k.do("tensor", lambda e, _c=cc, _d=dc, _pb=pb, _n=ncc:
                         e.matmul(_pb[0:_n, 0:48],
                                  winsb[:, _d * 2080 + _c * 128:
                                        _d * 2080 + _c * 128 + _n],
                                  xbT8[:, _d * 48:(_d + 1) * 48],
                                  start=(_d == 0), stop=(_d == 7)),
                         inc=(ch_pm if dc == 7 else None))
                k.wait(a, ch_pm, ch_pm.n)
                k.do(a, lambda e, _c=cc, _pb=pb, _n=ncc: e.activation(
                    zbT[0:_n, _c * 48:(_c + 1) * 48], _pb[0:_n, 0:48],
                    AF.Identity), inc=ch_pev)
            k.wait("tensor", ch_pev, ch_pev.n)

            for t in range(NT):
                k.wait("sync", ch_ln, ch_ln.n)
                k.dma("sync", sx[:, :], xin[t * 128:(t + 1) * 128, :],
                      ch_sld)
                k.wait(v, ch_sld, ch_sld.n)
                k.wait(v, ch_pm, ch_pm.n)
                ln_tile(k, sx, xln, sq, cols, ch_lna, ch_lnv, ch_done=ch_ln)
                k.wait("tensor", ch_ln, ch_ln.n)
                for dc in range(8):
                    sl = dc % 2
                    k.wait("tensor", ch_trev, ch_trev.n)
                    k.do("tensor", lambda e, _d=dc, _sl=sl: e.transpose(
                        ptrb[_sl][:, :], xln[:, _d * 128:(_d + 1) * 128],
                        C.identb[:, :]), inc=ch_tr)
                    k.wait(v, ch_tr, ch_tr.n)
                    k.do(v, lambda e, _d=dc, _sl=sl: e.tensor_copy(
                        xT8[:, _d * 128:(_d + 1) * 128], ptrb[_sl][:, :]),
                         inc=ch_trev)
                k.wait("tensor", ch_trev, ch_trev.n)
                k.wait("tensor", ch_ve, ch_ve.n)
                for cc in range(17):
                    ncc = 128 if cc < 16 else 32
                    pb = pz[cc % 2]
                    k.wait("tensor", ch_pev, max(0, ch_pev.n - 1))
                    for dc in range(8):
                        k.do("tensor", lambda e, _c=cc, _d=dc, _pb=pb,
                             _n=ncc: e.matmul(
                                 _pb[0:_n, :],
                                 winsb[:, _d * 2080 + _c * 128:
                                       _d * 2080 + _c * 128 + _n],
                                 xT8[:, _d * 128:(_d + 1) * 128],
                                 start=(_d == 0), stop=(_d == 7)),
                             inc=(ch_pm if dc == 7 else None))
                    k.wait(a, ch_pm, ch_pm.n)
                    k.do(a, lambda e, _c=cc, _pb=pb, _n=ncc: e.activation(
                        zT[0:_n, _c * 128:(_c + 1) * 128], _pb[0:_n, :],
                        AF.Identity), inc=ch_pev)
                k.wait(v, ch_pev, ch_pev.n)
                k.wait(v, ch_po, ch_po.n)
                k.wait(v, ch_se, ch_se.n)
                for cc in range(17):
                    ncc = 128 if cc < 16 else 32
                    zcs = zc[0:ncc, cc * 128:(cc + 1) * 128]
                    k.do(v, lambda e, _c=cc, _n=ncc, _z=zcs: e.tensor_scalar(
                        _z, zT[0:_n, _c * 128:(_c + 1) * 128],
                        wtap[0:_n, _c * 4 + 3:_c * 4 + 4],
                        cbc[0:_n, _c:_c + 1], ALU.mult, ALU.add))
                    for dl in range(1, 4):
                        j = 3 - dl
                        k.do(v, lambda e, _c=cc, _n=ncc, _d=dl, _j=j, _t=t:
                             e.scalar_tensor_tensor(
                                 zc[0:_n, _c * 128 + _d:(_c + 1) * 128],
                                 zT[0:_n, _c * 128:(_c + 1) * 128 - _d],
                                 wtap[0:_n, _c * 4 + _j:_c * 4 + _j + 1],
                                 zc[0:_n, _c * 128 + _d:(_c + 1) * 128],
                                 ALU.mult, ALU.add))
                        k.do(v, lambda e, _c=cc, _n=ncc, _d=dl, _j=j, _t=t:
                             e.scalar_tensor_tensor(
                                 zc[0:_n, _c * 128:_c * 128 + _d],
                                 zbT[0:_n, _c * 48 + _t * 3 + 3 - _d:
                                     _c * 48 + _t * 3 + 3],
                                 wtap[0:_n, _c * 4 + _j:_c * 4 + _j + 1],
                                 zc[0:_n, _c * 128:_c * 128 + _d],
                                 ALU.mult, ALU.add))
                k.do(v, lambda e: e.nop(), inc=ch_ve)
                k.wait(a, ch_ve, ch_ve.n)
                k.do(a, lambda e: e.activation(
                    zc[:, 1024:2048], zc[:, 1024:2048], AF.Silu), inc=ch_se)
                k.wait("tensor", ch_se, ch_se.n)
                k.wait("tensor", ch_po, ch_po.n)
                k.wait("tensor", ch_spg, ch_spg.n)
                for dc in range(17):
                    sl = dc % 2
                    k.wait("tensor", ch_trev, ch_trev.n)
                    if dc < 16:
                        k.do("tensor", lambda e, _d=dc, _sl=sl: e.transpose(
                            ptr[_sl][:, :], zc[:, _d * 128:(_d + 1) * 128],
                            C.identf[:, :]), inc=ch_tr)
                    else:
                        k.do("tensor", lambda e, _sl=sl: e.transpose(
                            ptr[_sl][:, 0:32], zc[0:32, 2048:2176],
                            C.identf[0:32, 0:32]), inc=ch_tr)
                    k.wait(v, ch_tr, ch_tr.n)
                    if dc < 8:
                        k.do(v, lambda e, _d=dc, _sl=sl: e.tensor_copy(
                            xval[:, _d * 128:(_d + 1) * 128], ptr[_sl][:, :]),
                             inc=ch_trev)
                    elif dc < 16:
                        k.do(v, lambda e, _d=dc, _sl=sl: e.tensor_copy(
                            gateb[:, (_d - 8) * 128:(_d - 7) * 128],
                            ptr[_sl][:, :]), inc=ch_trev)
                    else:
                        k.do(v, lambda e, _sl=sl: e.tensor_copy(
                            sraw[:, :], ptr[_sl][:, 0:32]), inc=ch_trev)
                k.wait("sync", ch_trev, ch_trev.n)
                k.dma("sync", spg[t], gateb[:, :], ch_spg)
                pp, tw = smc[:, 0:32], smc[:, 0:32]
                ssb = smc[:, 0:32]
                pwm = smc[:, 32:33]
                dcol = ccol[:, 2:3]
                pcol = ccol[:, 3:4]
                k.wait(v, ch_trev, ch_trev.n)
                k.do(v, lambda e, _t=t: e.tensor_scalar(
                    dcol, poscs[:, _t:_t + 1], -1.0, 4095.0, ALU.mult,
                    ALU.add))
                k.do(v, lambda e: e.tensor_tensor(
                    sraw[:, :], sraw[:, :], scb[:, :], ALU.mult))
                k.do(v, lambda e: e.tensor_scalar(
                    sraw[:, :], sraw[:, :], 20.0, None, ALU.min))
                k.do(v, lambda e: e.tensor_scalar(
                    sraw[:, :], sraw[:, :], -20.0, None, ALU.max))
                k.do(v, lambda e, _t=t: e.tensor_scalar(
                    t1b[:, 0:28], nspb[:, 0:28], dcol, None, ALU.mult))
                k.do(v, lambda e, _t=t: e.tensor_scalar(
                    t1b[:, 28:32], nspb[:, 28:32], poscs[:, _t:_t + 1],
                    None, ALU.mult), inc=ch_lnv)
                k.wait(a, ch_lnv, ch_lnv.n)
                k.do(a, lambda e: e.activation(sraw[:, :], sraw[:, :],
                                               AF.Exp))
                k.do(a, lambda e: e.activation(t1b[:, 0:32], t1b[:, 0:32],
                                               AF.Exp), inc=ch_lna)
                k.wait(v, ch_lna, ch_lna.n)
                k.do(v, lambda e: e.tensor_tensor(
                    smc[:, 0:32], sraw[:, :], t1b[:, 0:32], ALU.mult))
                k.do(v, lambda e: e.tensor_copy(pwb[:, :], smc[:, 0:32]),
                     inc=ch_lnv)
                k.wait("tensor", ch_lnv, ch_lnv.n)
                k.do("tensor", lambda e: e.transpose(
                    ptb[:, :], pwb[:, :], C.identb[:, :]), inc=ch_tr)
                k.wait(v, ch_tr, ch_tr.n)
                k.do(v, lambda e: e.tensor_copy(pwTb[:, :], ptb[:, :]),
                     inc=ch_trev)
                k.wait("tensor", ch_trev, ch_trev.n)
                k.wait("tensor", ch_po, ch_po.n)
                for half in range(2):
                    k.do("tensor", lambda e, _h=half: e.matmul(
                        pcu[_h][:, :], pwTb[:, :],
                        repl[:, _h * 512:(_h + 1) * 512],
                        start=True, stop=True),
                         inc=(ch_pm if half == 1 else None))
                k.wait(a, ch_pm, ch_pm.n)
                for half in range(2):
                    k.do(a, lambda e, _h=half: e.activation(
                        pwx[:, _h * 512:(_h + 1) * 512], pcu[_h][:, :],
                        AF.Identity), inc=(ch_pev if half == 1 else None))
                k.wait(a, ch_ve, ch_ve.n)
                k.do(a, lambda e: e.activation(t2b[:, :], xval[:, :],
                                               AF.Square), inc=ch_se)
                k.wait(v, ch_se, ch_se.n)
                k.wait(v, ch_pev, ch_pev.n)
                k.do(v, lambda e: e.tensor_scalar(
                    t1b[:, :], xval[:, :], wlcol[:, 1:2], wlcol[:, 0:1],
                    ALU.mult, ALU.add))
                k.do(v, lambda e: e.tensor_scalar(
                    t2b[:, :], t2b[:, :], wlcol[:, 2:3], None, ALU.mult))
                k.do(v, lambda e: e.tensor_tensor(
                    t1b[:, :], t1b[:, :], t2b[:, :], ALU.subtract))
                k.wait(v, ch_po, ch_po.n)
                k.do(v, lambda e: e.tensor_tensor(
                    ppw[:, :], t1b[:, :], pwx[:, :], ALU.mult), inc=ch_ve)
                mgre = mg[:, 0:4096].rearrange("p (k mh) -> p k mh", mh=128)
                mgim = mg[:, 4096:8192].rearrange("p (k mh) -> p k mh",
                                                  mh=128)
                for m in range(4):
                    k.wait(v, ch_po, ch_po.n)
                    k.wait(v, ch_se, max(0, ch_se.n))
                    k.do(v, lambda e, _m=m: e.tensor_tensor(
                        phib[:, :], xval[:, :],
                        thbc[:, _m * 1024:(_m + 1) * 1024], ALU.mult))
                    k.do(v, lambda e: e.tensor_scalar(
                        t2b[:, :], phib[:, :], 0.15915494309189535,
                        12582912.0, ALU.mult, ALU.add))
                    k.do(v, lambda e: e.tensor_scalar(
                        t2b[:, :], t2b[:, :], 12582912.0, None,
                        ALU.subtract))
                    k.do(v, lambda e: e.scalar_tensor_tensor(
                        phib[:, :], t2b[:, :], -6.283185307179586,
                        phib[:, :], ALU.mult, ALU.add), inc=ch_lnv)
                    k.wait(a, ch_lnv, ch_lnv.n)
                    k.wait(a, ch_po, ch_po.n)
                    k.do(a, lambda e: e.activation(sinb[:, :], phib[:, :],
                                                   AF.Sin))
                    k.do(a, lambda e: e.activation(t2b[:, :], phib[:, :],
                                                   AF.Abs), inc=ch_se)
                    k.wait(a, ch_se, ch_se.n)
                    k.do(a, lambda e: e.activation(
                        cosb[:, :], t2b[:, :], AF.Sin, bias=ccol[:, 0:1],
                        scale=-1.0), inc=ch_lna)
                    k.wait(g, ch_lna, ch_lna.n)
                    k.wait(g, ch_ve, ch_ve.n)
                    k.wait(g, ch_pm, ch_pm.n)
                    k.do(g, lambda e, _m=m: e.tensor_tensor(
                        mgre[:, :, _m * 32:(_m + 1) * 32],
                        ppw.rearrange("p (k h) -> p k h", h=32),
                        cosb.rearrange("p (k h) -> p k h", h=32), ALU.mult))
                    k.do(g, lambda e, _m=m: e.tensor_tensor(
                        mgim[:, :, _m * 32:(_m + 1) * 32],
                        ppw.rearrange("p (k h) -> p k h", h=32),
                        sinb.rearrange("p (k h) -> p k h", h=32), ALU.mult),
                         inc=ch_po)
                k.wait("tensor", ch_po, ch_po.n)
                k.wait("tensor", ch_pev, ch_pev.n)
                k.wait("tensor", ch_spn, ch_spn.n)
                for bank in range(16):
                    pb = pcu[bank % 2]
                    k.wait("tensor", ch_pev, max(0, ch_pev.n - 1))
                    for q in range(4):
                        cc = bank * 4 + q
                        k.do("tensor", lambda e, _c=cc, _q=q, _pb=pb:
                             e.matmul(_pb[:, _q * 128:(_q + 1) * 128],
                                      mg[:, _c * 128:(_c + 1) * 128],
                                      C.trib[:, :], start=True, stop=True),
                             inc=(ch_pm if q == 3 else None))
                    k.wait(a, ch_pm, ch_pm.n)
                    k.do(a, lambda e, _b=bank, _pb=pb: e.activation(
                        numbT[:, _b * 512:(_b + 1) * 512], _pb[:, :],
                        AF.Identity), inc=ch_pev)
                k.wait("tensor", ch_pev, ch_pev.n)
                k.wait("tensor", ch_lnv, ch_lnv.n)
                k.do("tensor", lambda e: e.matmul(
                    pz[0][0:32, :], pwb[:, :], C.trib[:, :],
                    start=True, stop=True), inc=ch_pm)
                k.wait(a, ch_pm, ch_pm.n)
                k.wait(a, ch_spg, max(0, ch_spg.n - 1))
                k.do(a, lambda e: e.activation(denT[:, :], pz[0][0:32, :],
                                               AF.Identity), inc=ch_pev)
                k.wait("sync", ch_pev, ch_pev.n)
                k.dma("sync", spd[t], denT[:, :], ch_spg)
                for q in range(17):
                    ncq = 512 if q < 16 else 32
                    k.wait("tensor", ch_trev, ch_trev.n)
                    if q < 16:
                        k.do("tensor", lambda e, _q=q, _n=ncq: e.matmul(
                            ptot[0:1, 0:_n], C.onesb_col[:, :],
                            mg[:, _q * 512:_q * 512 + _n],
                            start=True, stop=True), inc=ch_pm)
                    else:
                        k.do("tensor", lambda e, _n=ncq: e.matmul(
                            ptot[0:1, 0:_n], C.onesb_col[:, :], pwb[:, :],
                            start=True, stop=True), inc=ch_pm)
                    k.wait(v, ch_pm, ch_pm.n)
                    k.wait(v, ch_tot, ch_tot.n)
                    k.do(v, lambda e, _n=ncq: e.tensor_copy(
                        totr[0:1, 0:_n], ptot[0:1, 0:_n]), inc=ch_trev)
                    k.wait("sync", ch_trev, ch_trev.n)
                    k.dma("sync", agin[t:t + 1, q * 512:q * 512 + ncq],
                          totr[0:1, 0:ncq], ch_tot)
                k.wait("sync", ch_pev, ch_pev.n)
                k.dma("sync", spn[t], numbT[:, :], ch_spn)
            k.wait(g, ch_spn, ch_spn.n)
            k.wait(g, ch_tot, ch_tot.n)
            k.wait(g, ch_spg, ch_spg.n)
            k.barrier()
        cc2 = k.chan()
        k.do(g, lambda e: e.collective_compute(
            "AllGather", ALU.bypass, replica_groups=[list(range(NCORES))],
            ins=[agin], outs=[agout]), inc=cc2, auto=False)
        with contextlib.ExitStack() as psC:
            gsb = C.sb(psC, [128, 8224], BF16, "l0_gsb")
            carr = C.sb(psC, [16, 8320], F32, "l0_carr")
            ca16 = C.sb(psC, [16, 128], F32, "l0_ca16")
            pca = [C.ps(psC, [16, 512], F32, f"l0_pca{i}") for i in range(2)]
            pct = C.ps(psC, [128, 16], F32, "l0_pct")
            k.wait(g, cc2, 1)
            k.dma(g, gsb[:, :], agout[:, :], ch_ld)
            k.wait("tensor", ch_ld, ch_ld.n)
            for q in range(17):
                ncq = 512 if q < 16 else 32
                pb = pca[q % 2]
                k.wait("tensor", ch_pev, max(0, ch_pev.n - 1))
                k.do("tensor", lambda e, _q=q, _n=ncq, _pb=pb: e.matmul(
                    _pb[:, 0:_n], selmb[:, :],
                    gsb[:, _q * 512:_q * 512 + _n], start=True, stop=True),
                     inc=ch_pm)
                k.wait(a, ch_pm, ch_pm.n)
                k.do(a, lambda e, _q=q, _n=ncq, _pb=pb: e.activation(
                    carr[:, _q * 512:_q * 512 + _n], _pb[:, 0:_n],
                    AF.Identity), inc=ch_pev)
            k.wait(v, ch_pev, ch_pev.n)
            for cc in range(65):
                ncc = 128 if cc < 64 else 32
                k.wait(v, ch_tr, ch_tr.n)
                k.do(v, lambda e, _c=cc, _n=ncc: e.tensor_copy(
                    ca16[:, 0:_n], carr[:, _c * 128:_c * 128 + _n]),
                     inc=ch_trev)
                k.wait("tensor", ch_trev, ch_trev.n)
                k.wait("tensor", ch_lna, ch_lna.n)
                k.do("tensor", lambda e, _c=cc, _n=ncc: e.transpose(
                    pct[0:_n, :], ca16[:, 0:_n], C.identf[0:16, 0:16]),
                     inc=ch_tr)
                k.wait(a, ch_tr, ch_tr.n)
                k.do(a, lambda e, _c=cc, _n=ncc: e.activation(
                    carryT[0:_n, _c * 16:(_c + 1) * 16], pct[0:_n, :],
                    AF.Identity), inc=ch_lna)
            k.wait(v, ch_lna, ch_lna.n)
            k.barrier()
        with contextlib.ExitStack() as ps2:
            numb2 = C.sb(ps2, [128, 8192], BF16, "l0_numb2")
            denT2 = C.sb(ps2, [32, 128], F32, "l0_denT2")
            dinv = C.sb(ps2, [32, 128], F32, "l0_dinv")
            invden = C.sb(ps2, [128, 32], F32, "l0_invden")
            fac = C.sb(ps2, [128, 32], F32, "l0_fac")
            msqe = C.sb(ps2, [128, 32], F32, "l0_msqe")
            wre = C.sb(ps2, [128, 64], BF16, "l0_wre")
            wref = C.sb(ps2, [128, 64], F32, "l0_wref")
            nsc = C.sb(ps2, [128, 2], F32, "l0_nsc")
            sqb = [C.sb(ps2, [128, 128], BF16, f"l0_sqb{i}") for i in range(2)]
            yb = C.sb(ps2, [128, 1024], F32, "l0_y")
            ygb = C.sb(ps2, [128, 1024], BF16, "l0_yg")
            ygT = C.sb(ps2, [128, 1024], BF16, "l0_ygT")
            gate2 = C.sb(ps2, [128, 1024], BF16, "l0_gate2")
            xnew = C.sb(ps2, [128, 1024], F32, "l0_xnew")
            wout = C.sb(ps2, [128, 8192], BF16, "l0_wout")
            py = [C.ps(ps2, [128, 512], F32, f"l0_py{i}") for i in range(2)]
            pmq = C.ps(ps2, [128, 32], F32, "l0_pmq")
            ptr2 = C.ps(ps2, [128, 128], F32, "l0_ptr2")
            ptb2 = [C.ps(ps2, [128, 128], BF16, f"l0_ptb2{i}")
                    for i in range(2)]
            k.dma(g, wout[:, :].rearrange("p (c f) -> p c f", f=1024),
                  C.inp["l0a_wout"].rearrange("(c p) f -> p c f", p=128),
                  ch_ld)
            k.dma(g, wref[:, 0:32], C.inp["l0a_wre_p"][:, :], ch_ld)
            k.dma(g, wref[:, 32:64], C.inp["l0a_wim_p"][:, :], ch_ld)
            k.dma(g, nsc[:, 0:1], C.inp["l0a_nsre"][:, :], ch_ld)
            k.dma(g, nsc[:, 1:2], C.inp["l0a_nsim"][:, :], ch_ld)
            k.wait(v, ch_ld, ch_ld.n)
            k.do(v, lambda e: e.tensor_scalar(
                wre[:, 0:32], wref[:, 0:32], nsc[:, 0:1], None, ALU.mult))
            k.do(v, lambda e: e.tensor_scalar(
                wre[:, 32:64], wref[:, 32:64], nsc[:, 1:2], None, ALU.mult),
                 inc=ch_ve)
            k.wait("tensor", ch_ve, ch_ve.n)
            ch_ldn = k.dchan()
            for t in range(NT):
                k.wait(g, ch_pm, ch_pm.n)
                k.dma(g, numb2[:, :], spn[t], ch_ldn)
                k.dma(g, gate2[:, :], spg[t], ch_ld)
                k.dma(g, denT2[:, :], spd[t], ch_ld)
                k.wait("sync", ch_ln, ch_ln.n)
                k.dma("sync", sx[:, :], xin[t * 128:(t + 1) * 128, :],
                      ch_sld)
                k.wait(v, ch_ld, ch_ld.n)
                k.wait(v, ch_tr, ch_tr.n)
                k.do(v, lambda e, _t=t: e.tensor_scalar(
                    denT2[:, :], denT2[:, :],
                    carryT[0:32, 64 * 16 + _t:64 * 16 + _t + 1], None,
                    ALU.add))
                k.do(v, lambda e: e.tensor_scalar(
                    denT2[:, :], denT2[:, :], 1e-4, None, ALU.max))
                k.do(v, lambda e: e.reciprocal(dinv[:, :], denT2[:, :]),
                     inc=ch_lnv)
                k.wait("tensor", ch_lnv, ch_lnv.n)
                k.wait("tensor", ch_trev, ch_trev.n)
                k.wait("tensor", ch_ln, ch_ln.n)
                k.do("tensor", lambda e: e.transpose(
                    ptr2[0:128, 0:32], dinv[:, :], C.identf[0:32, 0:32]),
                     inc=ch_tr)
                k.wait(v, ch_tr, ch_tr.n)
                k.do(v, lambda e: e.tensor_copy(invden[:, :],
                                                ptr2[:, 0:32]), inc=ch_trev)
                k.wait(v, ch_ldn, ch_ldn.n)
                for kk in range(32):
                    k.wait(v, ch_pm, max(0, ch_pm.n - 2))
                    k.do(v, lambda e, _k=kk, _t=t: e.tensor_scalar(
                        numb2[:, _k * 128:(_k + 1) * 128],
                        numb2[:, _k * 128:(_k + 1) * 128],
                        carryT[:, _k * 16 + _t:_k * 16 + _t + 1], None,
                        ALU.add))
                    k.do(v, lambda e, _k=kk, _t=t: e.tensor_scalar(
                        numb2[:, 4096 + _k * 128:4096 + (_k + 1) * 128],
                        numb2[:, 4096 + _k * 128:4096 + (_k + 1) * 128],
                        carryT[:, (32 + _k) * 16 + _t:
                               (32 + _k) * 16 + _t + 1], None, ALU.add),
                         inc=ch_lnv)
                    k.wait(g, ch_lnv, ch_lnv.n)
                    k.wait(g, ch_pm, ch_pm.n)
                    k.do(g, lambda e, _k=kk: e.tensor_tensor(
                        sqb[_k % 2][:, :], numb2[:, _k * 128:(_k + 1) * 128],
                        numb2[:, _k * 128:(_k + 1) * 128], ALU.mult))
                    k.do(g, lambda e, _k=kk: e.tensor_tensor(
                        sqb[(_k + 1) % 2][:, :],
                        numb2[:, 4096 + _k * 128:4096 + (_k + 1) * 128],
                        numb2[:, 4096 + _k * 128:4096 + (_k + 1) * 128],
                        ALU.mult), inc=ch_po)
                    k.wait("tensor", ch_lnv, ch_lnv.n)
                    k.wait("tensor", ch_po, ch_po.n)
                    bk = py[kk // 16]
                    k.do("tensor", lambda e, _k=kk, _b=bk: e.matmul(
                        _b[:, (_k % 16) * 32:((_k % 16) + 1) * 32],
                        numb2[:, _k * 128:(_k + 1) * 128], wre[:, 0:32],
                        start=True, stop=False))
                    k.do("tensor", lambda e, _k=kk, _b=bk: e.matmul(
                        _b[:, (_k % 16) * 32:((_k % 16) + 1) * 32],
                        numb2[:, 4096 + _k * 128:4096 + (_k + 1) * 128],
                        wre[:, 32:64], start=False, stop=True))
                    k.do("tensor", lambda e, _k=kk: e.matmul(
                        pmq[:, _k:_k + 1], sqb[_k % 2][:, :],
                        C.onesb_col[:, :], start=True, stop=False))
                    k.do("tensor", lambda e, _k=kk: e.matmul(
                        pmq[:, _k:_k + 1], sqb[(_k + 1) % 2][:, :],
                        C.onesb_col[:, :], start=False, stop=True),
                         inc=ch_pm)
                k.wait(v, ch_pm, ch_pm.n)
                k.wait(v, ch_trev, ch_trev.n)
                k.do(v, lambda e: e.tensor_copy(msqe[:, :], pmq[:, :]))
                k.do(v, lambda e: e.tensor_tensor(
                    fac[:, :], invden[:, :], invden[:, :], ALU.mult))
                k.do(v, lambda e: e.tensor_tensor(
                    msqe[:, :], msqe[:, :], fac[:, :], ALU.mult),
                     inc=ch_lnv)
                k.wait(a, ch_lnv, ch_lnv.n)
                k.do(a, lambda e: e.activation(
                    msqe[:, :], msqe[:, :], AF.Sqrt, bias=ccol[:, 1:2],
                    scale=1.0 / 256.0), inc=ch_lna)
                k.wait(v, ch_lna, ch_lna.n)
                k.do(v, lambda e: e.reciprocal(fac[:, :], msqe[:, :]))
                k.do(v, lambda e: e.tensor_tensor(
                    fac[:, :], fac[:, :], invden[:, :], ALU.mult),
                     inc=ch_lnv)
                k.wait(v, ch_lnv, ch_lnv.n)
                for kk in range(32):
                    k.do(v, lambda e, _k=kk: e.tensor_scalar(
                        yb[:, _k * 32:(_k + 1) * 32],
                        py[_k // 16][:, (_k % 16) * 32:((_k % 16) + 1) * 32],
                        fac[:, _k:_k + 1], None, ALU.mult),
                         inc=(ch_ve if kk == 31 else None))
                k.wait(v, ch_ve, ch_ve.n)
                k.do(v, lambda e: e.tensor_tensor(
                    ygb[:, :], yb[:, :], gate2[:, :], ALU.mult), inc=ch_ve)
                k.wait("tensor", ch_ve, ch_ve.n)
                for dc in range(8):
                    sl = dc % 2
                    k.wait("tensor", ch_trev, ch_trev.n)
                    k.do("tensor", lambda e, _d=dc, _sl=sl: e.transpose(
                        ptb2[_sl][:, :], ygb[:, _d * 128:(_d + 1) * 128],
                        C.identb[:, :]), inc=ch_tr)
                    k.wait(v, ch_tr, ch_tr.n)
                    k.do(v, lambda e, _d=dc, _sl=sl: e.tensor_copy(
                        ygT[:, _d * 128:(_d + 1) * 128], ptb2[_sl][:, :]),
                         inc=ch_trev)
                k.wait("tensor", ch_trev, ch_trev.n)
                k.wait("tensor", ch_ln, ch_ln.n)
                for half in range(2):
                    pb = py[half]
                    for dc in range(8):
                        k.do("tensor", lambda e, _d=dc, _h=half, _pb=pb:
                             e.matmul(_pb[:, :],
                                      ygT[:, _d * 128:(_d + 1) * 128],
                                      wout[:, _d * 1024 + _h * 512:
                                           _d * 1024 + _h * 512 + 512],
                                      start=(_d == 0), stop=(_d == 7)),
                             inc=(ch_pm if dc == 7 else None))
                k.wait(v, ch_pm, ch_pm.n)
                k.wait(v, ch_sld, ch_sld.n)
                k.wait(v, ch_st, ch_st.n)
                for half in range(2):
                    k.do(v, lambda e, _h=half: e.tensor_tensor(
                        xnew[:, _h * 512:(_h + 1) * 512], py[_h][:, :],
                        sx[:, _h * 512:(_h + 1) * 512], ALU.add),
                         inc=(ch_ln if half == 1 else None))
                k.wait("sync", ch_ln, ch_ln.n)
                k.dma("sync", xout[t * 128:(t + 1) * 128, :], xnew[:, :],
                      ch_st)
        k.wait("sync", ch_st, ch_st.n)


def emit_l2attn(k, C, xin, xout):
    from concourse.aot_env import allow_ici_all_to_all
    with contextlib.ExitStack() as bs:
        sx = C.sb(bs, [128, 1024], F32, "l2a_sx")
        cols = C.sb(bs, [128, 8], F32, "l2a_cols")
        cqTf = C.sb(bs, [128, 8192], BF16, "l2a_cqTf")
        ckvTf = C.sb(bs, [128, 8192], BF16, "l2a_ckvTf")
        wu = C.sb(bs, [128, 1024], BF16, "l2a_wu")
        QT = C.sb(bs, [128, 16384], BF16, "l2a_QT")
        KT = C.sb(bs, [128, 16384], BF16, "l2a_KT")
        Vall = C.sb(bs, [128, 16384], BF16, "l2a_V")
        Srow = C.sb(bs, [128, 4096], F32, "l2a_S")
        Pn = C.sb(bs, [128, 4096], BF16, "l2a_P")
        ptev = [C.sb(bs, [128, 128], BF16, f"l2a_ptev{i}") for i in range(2)]
        maskc = C.sb(bs, [128, 128], F32, "l2a_mask")
        atile = [C.sb(bs, [64, 128], BF16, f"l2a_atile{i}") for i in range(2)]
        scols = C.sb(bs, [128, 4], F32, "l2a_scols")

        cqd = C.dram([4096, 512], BF16, "l2a_cqd")
        cqo = C.dram([4096, 512], BF16, "l2a_cqo")
        atd = C.dram([4096, 512], BF16, "l2a_atd")
        ato = C.dram([4096, 512], BF16, "l2a_ato")

        ch_ld = k.dchan()
        ch_sld = k.dchan()
        ch_spl = k.dchan()
        ch_lna, ch_lnv, ch_ln = k.chan(), k.chan(), k.chan()
        ch_tr, ch_trev = k.chan(), k.chan()
        ch_pm, ch_pev = k.chan(), k.chan()
        ch_ve = k.chan()
        ch_st = k.dchan()
        cc1, cc2 = k.chan(), k.chan()
        v = "vector"

        k.do("gpsimd", lambda e: e.memset(maskc[:, :], 0.0))
        k.do("gpsimd", lambda e: e.affine_select(
            out=maskc[:, :], in_=maskc[:, :], compare_op=ALU.is_ge,
            fill=-30000.0, base=0, pattern=[[-1, 128]], channel_multiplier=1),
             inc=ch_ve)

        with contextlib.ExitStack() as ps1:
            xln = C.sb(ps1, [128, 1024], BF16, "l2a_xln")
            sq = C.sb(ps1, [128, 1024], F32, "l2a_sq")
            xT8 = C.sb(ps1, [128, 1024], BF16, "l2a_xT8")
            wd = C.sb(ps1, [128, 4096], BF16, "l2a_wd")
            cqT = C.sb(ps1, [128, 4096], BF16, "l2a_cqT")
            ckvT = C.sb(ps1, [128, 4096], BF16, "l2a_ckvT")
            ptr = [C.ps(ps1, [128, 128], BF16, f"l2a_ptr{i}")
                   for i in range(2)]
            pc = [C.ps(ps1, [128, 128], F32, f"l2a_pc{i}") for i in range(2)]
            k.dma("gpsimd", wd[:, 0:2048].rearrange("p (c f) -> p c f",
                                                    f=256),
                  C.inp["l2a_wdq"].rearrange("(c p) f -> p c f", p=128),
                  ch_ld)
            k.dma("gpsimd", wd[:, 2048:4096].rearrange("p (c f) -> p c f",
                                                       f=256),
                  C.inp["l2a_wdkv"].rearrange("(c p) f -> p c f", p=128),
                  ch_ld)
            k.wait("tensor", ch_ld, ch_ld.n)
            for ti in range(NT):
                k.wait("sync", ch_ln, ch_ln.n)
                k.dma("sync", sx[:, :], xin[ti * 128:(ti + 1) * 128, :],
                      ch_sld)
                k.wait("vector", ch_sld, ch_sld.n)
                k.wait("vector", ch_tr, ch_tr.n)
                ln_tile(k, sx, xln, sq, cols, ch_lna, ch_lnv, ch_done=ch_ln)
                k.wait("tensor", ch_ln, ch_ln.n)
                k.wait("tensor", ch_pm, ch_pm.n)
                for c in range(8):
                    sl = c % 2
                    k.wait("tensor", ch_trev, ch_trev.n)
                    k.do("tensor", lambda e, _c=c, _sl=sl: e.transpose(
                        ptr[_sl][:, :], xln[:, _c * 128:(_c + 1) * 128],
                        C.identb[:, :]), inc=ch_tr)
                    k.wait("vector", ch_tr, ch_tr.n)
                    k.do("vector", lambda e, _c=c, _sl=sl: e.tensor_copy(
                        xT8[:, _c * 128:(_c + 1) * 128], ptr[_sl][:, :]),
                         inc=ch_trev)
                k.wait("tensor", ch_trev, ch_trev.n)
                for pj, dst in ((0, cqT), (1, ckvT)):
                    for ch2 in range(2):
                        pb = pc[ch2]
                        k.wait("tensor", ch_pev, max(0, ch_pev.n - 1))
                        for c in range(8):
                            k.do("tensor", lambda e, _pb=pb, _c=c, _pj=pj,
                                 _ch=ch2: e.matmul(
                                     _pb[:, :],
                                     wd[:, _pj * 2048 + _c * 256 +
                                        _ch * 128:_pj * 2048 + _c * 256 +
                                        (_ch + 1) * 128],
                                     xT8[:, _c * 128:(_c + 1) * 128],
                                     start=(_c == 0), stop=(_c == 7)),
                                 inc=(ch_pm if c == 7 else None))
                        k.wait("scalar", ch_pm, ch_pm.n)
                        k.do("scalar", lambda e, _pb=pb, _d=dst, _ch=ch2,
                             _ti=ti: e.activation(
                                 _d[:, _ch * 2048 + _ti * 128:
                                    _ch * 2048 + (_ti + 1) * 128],
                                 _pb[:, :], AF.Identity), inc=ch_pev)
            k.wait("sync", ch_pev, ch_pev.n)
            for cstar in range(NCORES):
                bstar = cstar // 2
                for pj, srcm in ((0, cqT), (1, ckvT)):
                    for ch2 in range(2):
                        k.dma("sync",
                              cqd[cstar * 512 + pj * 256 + ch2 * 128:
                                  cstar * 512 + pj * 256 + (ch2 + 1) * 128,
                                  :],
                              srcm[:, ch2 * 2048 + bstar * 512:
                                   ch2 * 2048 + (bstar + 1) * 512], ch_spl)
            k.wait("gpsimd", ch_spl, ch_spl.n)
            with allow_ici_all_to_all():
                k.do("gpsimd", lambda e: e.collective_compute(
                    "AllToAll", ALU.bypass,
                    replica_groups=[list(range(NCORES))],
                    ins=[cqd], outs=[cqo]), inc=cc1, auto=False)

        with contextlib.ExitStack() as ps2:
            pqk = [C.ps(ps2, [128, 512], F32, f"l2a_pqk{i}")
                   for i in range(2)]
            pS = [C.ps(ps2, [128, 512], F32, f"l2a_pS{i}") for i in range(2)]
            ptb = [C.ps(ps2, [128, 128], BF16, f"l2a_ptb{i}")
                   for i in range(2)]
            pat = C.ps(ps2, [64, 128], F32, "l2a_pat")
            k.wait("gpsimd", cc1, 1)
            for r in range(NCORES):
                for pj, dstf in ((0, cqTf), (1, ckvTf)):
                    for ch2 in range(2):
                        for j in range(4):
                            k.dma("gpsimd",
                                  dstf[:, ch2 * 4096 + (r + 8 * j) * 128:
                                       ch2 * 4096 + (r + 8 * j + 1) * 128],
                                  cqo[r * 512 + pj * 256 + ch2 * 128:
                                      r * 512 + pj * 256 + (ch2 + 1) * 128,
                                      j * 128:(j + 1) * 128], ch_ld)
            for wname, dstT, scl, srcf in (("wuq", QT, 0.125, cqTf),
                                           ("wuk", KT, 1.0, ckvTf)):
                k.wait("gpsimd", ch_pm, ch_pm.n)
                k.dma("gpsimd",
                      wu[:, :].rearrange("p (c f) -> p c f", f=512),
                      C.inp[f"l2a_{wname}"].rearrange("(c p) f -> p c f",
                                                      p=128), ch_ld)
                k.wait("tensor", ch_ld, ch_ld.n)
                for p2 in range(4):
                    for tc in range(8):
                        pb = pqk[tc % 2]
                        k.wait("tensor", ch_pev, max(0, ch_pev.n - 1))
                        for c in range(2):
                            k.do("tensor", lambda e, _pb=pb, _c=c, _p2=p2,
                                 _tc=tc, _sf=srcf: e.matmul(
                                     _pb[:, :],
                                     wu[:, _c * 512 + _p2 * 128:
                                        _c * 512 + (_p2 + 1) * 128],
                                     _sf[:, _c * 4096 + _tc * 512:
                                         _c * 4096 + (_tc + 1) * 512],
                                     start=(_c == 0), stop=(_c == 1)),
                                 inc=(ch_pm if c == 1 else None))
                        k.wait("scalar", ch_pm, ch_pm.n)
                        k.do("scalar", lambda e, _pb=pb, _p2=p2, _tc=tc,
                             _d=dstT, _s=scl: e.activation(
                                 _d[:, _p2 * 4096 + _tc * 512:
                                    _p2 * 4096 + (_tc + 1) * 512],
                                 _pb[:, :], AF.Identity, scale=_s),
                             inc=ch_pev)
            k.wait("gpsimd", ch_pm, ch_pm.n)
            k.dma("gpsimd", wu[:, :].rearrange("p (c f) -> p c f", f=512),
                  C.inp["l2a_wuv"].rearrange("(c p) f -> p c f", p=128),
                  ch_ld)
            k.wait("tensor", ch_ld, ch_ld.n)
            for g in range(32):
                pb = pqk[g % 2]
                k.wait("tensor", ch_pev, max(0, ch_pev.n - 1))
                for c in range(2):
                    k.do("tensor", lambda e, _pb=pb, _c=c, _g=g: e.matmul(
                        _pb[:, :],
                        ckvTf[:, _c * 4096 + _g * 128:
                              _c * 4096 + (_g + 1) * 128],
                        wu[:, _c * 512:(_c + 1) * 512],
                        start=(_c == 0), stop=(_c == 1)),
                         inc=(ch_pm if c == 1 else None))
                k.wait("scalar", ch_pm, ch_pm.n)
                k.do("scalar", lambda e, _pb=pb, _g=g: e.activation(
                    Vall[:, _g * 512:(_g + 1) * 512], _pb[:, :],
                    AF.Identity), inc=ch_pev)
            for h in range(8):
                p2, off = h // 2, 64 * (h % 2)
                for qt in range(32):
                    nk = qt + 1
                    nkc = (nk * 128 + 511) // 512
                    k.wait("tensor", ch_pev, ch_pev.n)
                    for kc in range(nkc):
                        ncols = min(512, nk * 128 - kc * 512)
                        pb = pS[kc % 2]
                        k.wait("tensor", ch_lna, ch_lna.n)
                        k.do("tensor", lambda e, _pb=pb, _p2=p2, _o=off,
                             _qt=qt, _kc=kc, _n=ncols: e.matmul(
                                 _pb[:, 0:_n],
                                 QT[_o:_o + 64,
                                    _p2 * 4096 + _qt * 128:
                                    _p2 * 4096 + (_qt + 1) * 128],
                                 KT[_o:_o + 64,
                                    _p2 * 4096 + _kc * 512:
                                    _p2 * 4096 + _kc * 512 + _n],
                                 start=True, stop=True), inc=ch_pm)
                        k.wait("scalar", ch_pm, ch_pm.n)
                        k.do("scalar", lambda e, _pb=pb, _kc=kc, _n=ncols:
                             e.activation(Srow[:, _kc * 512:_kc * 512 + _n],
                                          _pb[:, 0:_n], AF.Identity),
                             inc=ch_lna)
                    k.wait(v, ch_lna, ch_lna.n)
                    k.do(v, lambda e, _qt=qt: e.tensor_tensor(
                        Srow[:, _qt * 128:(_qt + 1) * 128],
                        Srow[:, _qt * 128:(_qt + 1) * 128],
                        maskc[:, :], ALU.add))
                    mx, nmx, sume, rs = (scols[:, 0:1], scols[:, 1:2],
                                         scols[:, 2:3], scols[:, 3:4])
                    k.do(v, lambda e, _n=nk * 128: e.reduce_max(
                        mx, Srow[:, 0:_n], axis=AX.X))
                    k.do(v, lambda e: e.tensor_scalar(
                        nmx, mx, -1.0, None, ALU.mult), inc=ch_lnv)
                    k.wait("scalar", ch_lnv, ch_lnv.n)
                    k.wait("scalar", ch_tr, ch_tr.n)
                    k.do("scalar", lambda e, _n=nk * 128: e.activation(
                        Pn[:, 0:_n], Srow[:, 0:_n], AF.Exp, bias=nmx,
                        accum_out=sume), inc=ch_lna)
                    k.wait(v, ch_lna, ch_lna.n)
                    k.do(v, lambda e: e.reciprocal(rs, sume))
                    k.do(v, lambda e, _n=nk * 128: e.tensor_scalar(
                        Pn[:, 0:_n], Pn[:, 0:_n], rs, None, ALU.mult),
                         inc=ch_ve)
                    k.wait("tensor", ch_ve, ch_ve.n)
                    for kt in range(nk):
                        sl = kt % 2
                        k.wait("tensor", ch_trev, ch_trev.n)
                        k.do("tensor", lambda e, _kt=kt, _sl=sl:
                             e.transpose(ptb[_sl][:, :],
                                         Pn[:, _kt * 128:(_kt + 1) * 128],
                                         C.identb[:, :]), inc=ch_tr)
                        k.wait(v, ch_tr, ch_tr.n)
                        k.do(v, lambda e, _sl=sl: e.tensor_copy(
                            ptev[_sl][:, :], ptb[_sl][:, :]), inc=ch_trev)
                        k.wait("tensor", ch_trev, ch_trev.n)
                        k.do("tensor", lambda e, _kt=kt, _sl=sl, _h=h, _nk=nk:
                             e.matmul(pat[:, :],
                                      Vall[:, _kt * 512 + _h * 64:
                                           _kt * 512 + (_h + 1) * 64],
                                      ptev[_sl][:, :],
                                      start=(_kt == 0), stop=(_kt == _nk - 1)),
                             inc=(ch_pm if kt == nk - 1 else None))
                    asl = qt % 2
                    cstar, j = qt % 8, qt // 8
                    k.wait("scalar", ch_pm, ch_pm.n)
                    k.wait("scalar", ch_spl, ch_spl.n)
                    k.do("scalar", lambda e, _a=asl: e.activation(
                        atile[_a][:, :], pat[:, :], AF.Identity),
                         inc=ch_pev)
                    k.wait("sync", ch_pev, ch_pev.n)
                    k.dma("sync",
                          atd[cstar * 512 + p2 * 128 + off:
                              cstar * 512 + p2 * 128 + off + 64,
                              j * 128:(j + 1) * 128],
                          atile[asl][:, :], ch_spl)
            k.wait("gpsimd", ch_spl, ch_spl.n)
            with allow_ici_all_to_all():
                k.do("gpsimd", lambda e: e.collective_compute(
                    "AllToAll", ALU.bypass,
                    replica_groups=[list(range(NCORES))],
                    ins=[atd], outs=[ato]), inc=cc2, auto=False)

        with contextlib.ExitStack() as ps3:
            wo = C.sb(ps3, [128, 8192], BF16, "l2a_wosb")
            aTt = C.sb(ps3, [128, 1024], BF16, "l2a_aTt")
            xnew = C.sb(ps3, [128, 1024], F32, "l2a_xnew")
            pw = [C.ps(ps3, [128, 512], F32, f"l2a_pw{i}") for i in range(2)]
            k.wait("gpsimd", cc2, 1)
            k.dma("gpsimd", wo[:, :].rearrange("p (c f) -> p c f", f=1024),
                  C.inp["l2a_wo"].rearrange("(c p) f -> p c f", p=128),
                  ch_ld)
            for ti in range(NT):
                b, j = ti // 4, ti % 4
                k.wait("gpsimd", ch_pm, ch_pm.n)
                for c in range(8):
                    rsrc = 2 * b + c // 4
                    k.dma("gpsimd", aTt[:, c * 128:(c + 1) * 128],
                          ato[rsrc * 512 + (c % 4) * 128:
                              rsrc * 512 + (c % 4 + 1) * 128,
                              j * 128:(j + 1) * 128], ch_ld)
                k.wait("tensor", ch_ld, ch_ld.n)
                k.wait("tensor", ch_ln, ch_ln.n)
                for dc in range(2):
                    pb = pw[dc]
                    for c in range(8):
                        k.do("tensor", lambda e, _pb=pb, _c=c, _dc=dc:
                             e.matmul(_pb[:, :],
                                      aTt[:, _c * 128:(_c + 1) * 128],
                                      wo[:, _c * 1024 + _dc * 512:
                                         _c * 1024 + _dc * 512 + 512],
                                      start=(_c == 0), stop=(_c == 7)),
                             inc=(ch_pm if c == 7 else None))
                k.wait("sync", ch_st, ch_st.n)
                k.dma("sync", sx[:, :], xin[ti * 128:(ti + 1) * 128, :],
                      ch_sld)
                k.wait(v, ch_pm, ch_pm.n)
                k.wait(v, ch_sld, ch_sld.n)
                for dc in range(2):
                    k.do(v, lambda e, _dc=dc: e.tensor_tensor(
                        xnew[:, _dc * 512:(_dc + 1) * 512], pw[_dc][:, :],
                        sx[:, _dc * 512:(_dc + 1) * 512], ALU.add),
                         inc=(ch_ln if dc == 1 else None))
                k.wait("sync", ch_ln, ch_ln.n)
                k.dma("sync", xout[ti * 128:(ti + 1) * 128, :], xnew[:, :],
                      ch_st)
        k.wait("sync", ch_st, ch_st.n)


BLOCK_EMITTERS = {"l3ffn": emit_l3ffn, "l1ffn": emit_l1ffn,
                  "l0ffn": emit_l0ffn, "l2ffn": emit_l2ffn,
                  "l1attn": emit_l1attn, "l3attn": emit_l3attn,
                  "l2attn": emit_l2attn, "l0attn": emit_l0attn}

BLOCK_ORDER = ["l0attn", "l0ffn", "l1attn", "l1ffn",
               "l2attn", "l2ffn", "l3attn", "l3ffn"]


def stage_shared(inp):
    st = {}
    st["ident128"] = np.eye(128, dtype=np.float32)
    st["l3f_wfcT"] = np.ascontiguousarray(np.asarray(inp["L3_ffn_W_fc"]).T)
    st["l3f_wpjT"] = np.ascontiguousarray(np.asarray(inp["L3_ffn_W_proj"]).T)
    for nm, src_ in (("wfT", "wf"), ("wiT", "wi"), ("wcT", "wc"),
                     ("ufT", "uf"), ("uiT", "ui"), ("ucT", "uc"),
                     ("woT", "wo"), ("uoT", "uo")):
        st[f"l1f_{nm}"] = np.ascontiguousarray(
            np.asarray(inp[f"L1_ffn_{src_}"]).T)
    for nm in ("bf", "bi", "bc", "bo"):
        st[f"l1f_{nm}"] = np.asarray(inp[f"L1_ffn_{nm}"]).reshape(1, 1024)
    st["l1f_inc"] = np.asarray(inp["L1_ffn_in_c"]).reshape(1, 1024)
    st["moe0_wr"] = np.asarray(inp["L0_ffn_w_router"])
    st["moe0_w1"] = np.asarray(inp["L0_ffn_w1"])
    st["moe0_b1t"] = np.ascontiguousarray(
        np.asarray(inp["L0_ffn_b1"]).reshape(8, 32, 128).transpose(0, 2, 1))
    st["moe0_w2"] = np.asarray(inp["L0_ffn_w2"])
    st["moe0_b2"] = np.asarray(inp["L0_ffn_b2"])
    st["moe2_wr"] = np.asarray(inp["L2_ffn_Wr"])
    st["moe2_br"] = np.asarray(inp["L2_ffn_br"]).reshape(1, 8)
    st["moe2_w1"] = np.asarray(inp["L2_ffn_W1"])
    st["moe2_b1t"] = np.ascontiguousarray(
        np.asarray(inp["L2_ffn_b1"]).reshape(8, 16, 128).transpose(0, 2, 1))
    st["moe2_w2"] = np.asarray(inp["L2_ffn_W2"])
    st["moe2_b2"] = np.asarray(inp["L2_ffn_b2"])
    dd = np.arange(1024)
    ph = 2.0 * np.pi * np.outer(dd, dd) / 1024.0
    st["dftC"] = np.cos(ph).astype(np.float32)
    st["dftS"] = np.sin(ph).astype(np.float32)
    st["l1a_wk"] = np.asarray(inp["L1_attn_Wk"])
    st["l1a_wv"] = np.asarray(inp["L1_attn_Wv"])
    st["l1a_wq"] = np.asarray(inp["L1_attn_Wq"])
    st["l1a_bk"] = np.ascontiguousarray(np.asarray(inp["L1_attn_bk"]).reshape(8, 128).T)
    st["l1a_bv"] = np.ascontiguousarray(np.asarray(inp["L1_attn_bv"]).reshape(8, 128).T)
    st["l1a_bq"] = np.ascontiguousarray(np.asarray(inp["L1_attn_bq"]).reshape(8, 128).T)
    st["l3a_wattn"] = np.asarray(inp["L3_attn_W_attn"])
    st["l3a_wproj"] = np.asarray(inp["L3_attn_W_proj"])
    st["l2a_wdq"] = np.asarray(inp["L2_attn_W_dq"])
    st["l2a_wdkv"] = np.asarray(inp["L2_attn_W_dkv"])
    st["l2a_wo"] = np.asarray(inp["L2_attn_W_o"])
    st["l0a_win"] = np.asarray(inp["L0_attn_in_proj_w"])
    cw = np.asarray(inp["L0_attn_conv_w"])[:, 0, :]
    wt = np.zeros((128, 68), np.float32)
    cb = np.zeros((128, 17), np.float32)
    cbv = np.asarray(inp["L0_attn_conv_b"])
    for cc in range(17):
        n = 128 if cc < 16 else 32
        wt[0:n, cc * 4:(cc + 1) * 4] = cw[:, cc * 128:cc * 128 + n].T
        cb[0:n, cc] = cbv[cc * 128:cc * 128 + n]
    st["l0a_wtap"] = wt
    st["l0a_cb"] = cb
    th = np.asarray(inp["L0_attn_theta"]).reshape(32, 32, 4)
    st["l0a_ths"] = np.ascontiguousarray(
        th.transpose(2, 0, 1).reshape(1, 4096))
    st["l0a_dlg"] = np.asarray(inp["L0_attn_deriv_logits"]).reshape(1, 3)
    st["l0a_dsl"] = np.asarray(inp["L0_attn_decay_slopes"]).reshape(1, 28)
    st["l0a_asl"] = np.asarray(inp["L0_attn_anchor_slopes"]).reshape(1, 4)
    st["l0a_ssc"] = np.asarray(inp["L0_attn_score_scale"]).reshape(1, 32)
    perm = np.arange(128).reshape(4, 32)
    perm = (perm % 32) * 4 + np.arange(4).reshape(4, 1)
    perm = perm.reshape(128)
    st["l0a_wre_p"] = np.ascontiguousarray(
        np.asarray(inp["L0_attn_W_re"])[perm, :])
    st["l0a_wim_p"] = np.ascontiguousarray(
        np.asarray(inp["L0_attn_W_im"])[perm, :])
    ns = np.asarray(inp["L0_attn_norm_scale"])
    st["l0a_nsre"] = np.ascontiguousarray(ns[:128][perm].reshape(128, 1))
    st["l0a_nsim"] = np.ascontiguousarray(ns[128:][perm].reshape(128, 1))
    st["repl32"] = np.repeat(np.eye(32, dtype=np.float32), 32, axis=1)
    st["l0a_wout"] = np.asarray(inp["L0_attn_out_proj_w"])
    return st


def stage_percore(inp, c):
    d = {}
    selm = np.zeros((128, 16), dtype=np.float32)
    for r in range(8):
        for bb in range(4):
            for jp in range(4):
                row = r * 16 + bb * 4 + jp
                gp = r + 8 * jp
                for j in range(4):
                    if gp < c + 8 * j:
                        selm[row, bb * 4 + j] = 1.0
    d["l1a_selm"] = selm
    posc = np.zeros((128, 16), np.float32)
    selx = np.zeros((3, 128, 48), np.float32)
    for t in range(16):
        b, j = t // 4, t % 4
        gg = c + 8 * j
        posc[:, t] = gg * 128 + np.arange(128)
        if gg > 0:
            cp, jp = (c - 1, j) if c > 0 else (7, j - 1)
            for dt in range(3):
                row = cp * 48 + (b * 4 + jp) * 3 + dt
                selx[row // 128, row % 128, t * 3 + dt] = 1.0
    d["l0a_posc"] = posc
    d["l0a_selx"] = np.ascontiguousarray(selx.transpose(1, 0, 2))
    h0 = (c % 2) * 8
    d["l2a_wuq"] = np.ascontiguousarray(
        np.asarray(inp["L2_attn_W_uq"])[:, h0 * 64:(h0 + 8) * 64])
    d["l2a_wuk"] = np.ascontiguousarray(
        np.asarray(inp["L2_attn_W_uk"])[:, h0 * 64:(h0 + 8) * 64])
    d["l2a_wuv"] = np.ascontiguousarray(
        np.asarray(inp["L2_attn_W_uv"])[:, h0 * 64:(h0 + 8) * 64])
    return d


def shard_x(x):
    xt = np.asarray(x, dtype=np.float32).reshape(B, 32, 128, D)
    return [np.ascontiguousarray(xt[:, c::8, :, :].reshape(ROWS, D))
            for c in range(NCORES)]


def unshard_x(shards):
    x = np.zeros((B, 32, 128, D), dtype=np.float32)
    for c in range(NCORES):
        x[:, c::8, :, :] = np.asarray(shards[c], dtype=np.float32
                                      ).reshape(B, TPB, 128, D)
    return x.reshape(B, S, D)


def build_program(block_names, input_shapes):
    nc = bass.Bass("TRN2", target_bir_lowering=False)
    k = K(nc)
    with contextlib.ExitStack() as stk:
        C = Ctx(nc, k, stk)
        for name, (shape, dt) in input_shapes.items():
            C.inp[name] = nc.dram_tensor(name, list(shape), dt,
                                         kind="ExternalInput").ap()
        xA = nc.dram_tensor("xA", [ROWS, D], F32, kind="ExternalInput").ap()
        xout = nc.dram_tensor("xout", [ROWS, D], F32,
                              kind="ExternalOutput").ap()
        xB = C.dram([ROWS, D], F32, "xB")
        xC = C.dram([ROWS, D], F32, "xC")
        emit_prelude(k, C, stk)
        cur = xA
        n = len(block_names)
        for i, bn in enumerate(block_names):
            dst = xout if i == n - 1 else (xB if i % 2 == 0 else xC)
            BLOCK_EMITTERS[bn](k, C, cur, dst)
            k.barrier()
            cur = dst
        block = stk.enter_context(nc.Block())
        k.play(block)
    return nc


def make_in_maps(inp, blocks=None):
    shared = stage_shared(inp)
    xs = shard_x(inp["L0_attn_x"])
    percore = [stage_percore(inp, c) for c in range(NCORES)]
    input_shapes = {n: (v.shape, F32) for n, v in shared.items()}
    for n, v in percore[0].items():
        input_shapes[n] = (v.shape, F32)
    in_maps = []
    for c in range(NCORES):
        m = dict(shared)
        m.update(percore[c])
        m["xA"] = xs[c]
        in_maps.append(m)
    return input_shapes, in_maps


def kernel(**inp):
    blocks = [b for b in BLOCK_ORDER if b in BLOCK_EMITTERS]
    input_shapes, in_maps = make_in_maps(inp)
    nc = build_program(blocks, input_shapes)
    res = run_bass_kernel_spmd(nc, in_maps, core_ids=list(range(NCORES)))
    return unshard_x([res.results[c]["xout"] for c in range(NCORES)])
